# version 07 (final)
# speedup vs baseline: 1.6045x; 1.0011x over previous
import contextlib
import numpy as np
import ml_dtypes
import concourse.bass as bass
import concourse.mybir as mybir
from concourse.bass_utils import run_bass_kernel_spmd
from concourse.bass_types import AP




ENGINES = ("tensor", "vector", "scalar", "gpsimd", "sync")


class Step:
    __slots__ = ("engine", "fn", "reads", "writes", "channel", "waits", "inc", "tag")

    def __init__(self, engine, fn, reads, writes, channel, tag):
        self.engine = engine
        self.fn = fn
        self.reads = reads
        self.writes = writes
        self.channel = channel
        self.waits = []
        self.inc = None
        self.tag = tag


class Tape:
    def __init__(self):
        self.steps = []
        self.counts = {e: 0 for e in ENGINES}
        self.ch_counts = {}
        self.writers = {}
        self.readers = {}

    def barrier(self):
        snap = dict(self.counts)
        snap.update(self.ch_counts)
        self._barrier = (snap, set())

    def step(self, engine, fn, reads=(), writes=(), channel=None, tag=""):
        st = Step(engine, fn, list(reads), list(writes), channel, tag)
        if channel is not None:
            self.ch_counts.setdefault(channel, 0)
        waits = {}
        bar = getattr(self, "_barrier", None)
        if bar is not None and engine not in bar[1]:
            bar[1].add(engine)
            for tl, cnt in bar[0].items():
                if cnt > 0:
                    waits[tl] = max(waits.get(tl, 0), cnt)
            if len(bar[1]) == len(ENGINES):
                self._barrier = None

        def need(dep_tl, cnt, dep_is_dma):
            if dep_is_dma:
                cnt = self.ch_counts[dep_tl]
            else:
                if dep_tl == engine == "tensor" and channel is None:
                    return
            waits[dep_tl] = max(waits.get(dep_tl, 0), cnt)

        for b in st.reads:
            for wtl, (wcnt, wisd) in self.writers.get(b, {}).items():
                need(wtl, wcnt, wisd)
        for b in st.writes:
            if not b.endswith("+"):
                for wtl, (wcnt, wisd) in self.writers.get(b, {}).items():
                    need(wtl, wcnt, wisd)
            for rd in self.readers.get(b, ()):
                need(*rd)
        st.waits = sorted(waits.items())

        if channel is not None:
            self.ch_counts[channel] += 16
            st.inc = (channel, 16)
            tl, cnt, isd = channel, self.ch_counts[channel], True
        else:
            self.counts[engine] += 1
            st.inc = (engine, 1)
            tl, cnt, isd = engine, self.counts[engine], False

        for b in st.reads:
            self.readers.setdefault(b, []).append((tl, cnt, isd))
        for b in st.writes:
            w = self.writers.setdefault(b, {})
            w[tl] = (cnt, isd)
            self.readers[b] = []
        self.steps.append(st)
        return st

    def t(self, fn, reads=(), writes=(), tag=""):
        return self.step("tensor", fn, reads, writes, None, tag)

    def v(self, fn, reads=(), writes=(), tag=""):
        return self.step("vector", fn, reads, writes, None, tag)

    def s(self, fn, reads=(), writes=(), tag=""):
        return self.step("scalar", fn, reads, writes, None, tag)

    def g(self, fn, reads=(), writes=(), tag=""):
        return self.step("gpsimd", fn, reads, writes, None, tag)

    def dma(self, queue, channel, fn, reads=(), writes=(), tag=""):
        return self.step(queue, fn, reads, writes, channel, tag)

    def emit(self, nc, stack):
        sems = {}
        for e, c in self.counts.items():
            if c > 0:
                sems[e] = stack.enter_context(nc.semaphore("sem_" + e))
        for ch, c in self.ch_counts.items():
            if c > 0:
                sems[ch] = stack.enter_context(nc.semaphore("sem_" + ch))
        block = stack.enter_context(nc.Block())
        per_engine = {e: [] for e in ENGINES}
        for st in self.steps:
            per_engine[st.engine].append(st)

        totals = dict(self.counts)
        totals.update(self.ch_counts)

        def make_body(eng_name, steps):
            def body(eng):
                for st in steps:
                    for (tl, cnt) in st.waits:
                        eng.wait_ge(sems[tl], cnt)
                    last = st.fn(eng)
                    if last is None:
                        raise RuntimeError(f"step {st.tag!r} emit_fn returned None")
                    tl, k = st.inc
                    last.then_inc(sems[tl], k)
                for tl, total in totals.items():
                    if total > 0:
                        eng.wait_ge(sems[tl], total)
            return body

        for e in ENGINES:
            getattr(block, e)(make_body(e, per_engine[e]))
        return sems


F32 = mybir.dt.float32
BF16 = mybir.dt.bfloat16
AF = mybir.ActivationFunctionType
ALU = mybir.AluOpType

D = 1024
TB = 2048
NB = 2
NPAIR = 8
NROPE = 1369
PRE = TB - NROPE
QC = 512
NQC = TB // QC
NKC = 16
EXPSHIFT = -30.0


def emit_l0attn(nc, T, ps, ins, x_in, x_out, n_cores=8):
    import contextlib as _ctx
    bstack = _ctx.ExitStack()
    ent = bstack.enter_context
    xT = x_in
    x1T = x_out
    w_qkv = ins['w_qkv']
    bqk_cols = ins['bqk_cols']
    bv_cols = ins['bv_cols']
    bproj_row = ins['bproj_row']
    w_proj = ins['w_proj']
    sinT = ins['sinT']
    cosT = ins['cosT']

    sb = lambda name, shape, dt: ent(nc.sbuf_tensor('l0a_' + name, shape, dt))

    xn_b = sb("xn_b", [128, 8 * TB], BF16)
    pvdiv = sb("pvdiv", [128, NPAIR * TB], BF16)
    qsl = sb("qsl", [128, TB], BF16)
    ksl = sb("ksl", [128, TB], BF16)
    vsl = sb("vsl", [128, 16 * 130], BF16)
    xrot = [sb(f"xrot{i}", [128, TB], F32) for i in range(2)]
    xbrot = [sb(f"xbrot{i}", [128, QC], BF16) for i in range(2)]
    xqrot = [sb(f"xqrot{i}", [128, QC], BF16) for i in range(2)]
    mean_s = sb("mean_s", [128, TB], F32)
    rstd_s = sb("rstd_s", [128, TB], F32)
    stat_t = sb("stat_t", [128, QC], F32)
    qraw = sb("qraw", [128, TB], F32)
    rtmp = sb("rtmp", [128, NROPE], F32)
    rrot = sb("rrot", [128, NROPE], F32)
    sinsg = sb("sinsg", [128, NROPE], F32)
    cos2 = sb("cos2", [128, NROPE], F32)
    wqc = sb("wqc", [128, 8 * 128], BF16)
    wkc = sb("wkc", [128, 8 * 128], BF16)
    wvc = sb("wvc", [128, 8 * 128], BF16)
    wp_bf = sb("wp_bf", [128, 8 * D], BF16)
    bqk_s = sb("bqk_s", [128, 16], F32)
    bv_f = sb("bv_f", [128, 8], F32)
    bv_bf = sb("bv_bf", [128, 8], BF16)
    bproj_s = sb("bproj_s", [1, D], F32)
    crow_f = sb("crow_f", [1, D], F32)
    crow_bf = sb("crow_bf", [1, D], BF16)
    ones_bf = sb("ones_bf", [128, 128], BF16)
    ones1_f = sb("ones1_f", [1, 64], F32)
    ones1_bf = sb("ones1_bf", [1, QC], BF16)
    konst = sb("konst", [128, 2], F32)
    ebuf = sb("ebuf", [128, 2 * 1024], BF16)
    rrow = sb("rrow", [1, QC], F32)
    rrb_s = sb("rrb_s", [64, 2 * QC], F32)
    import contextlib as _ctx


    T.dma("sync", "const", lambda e: e.dma_start(out=bqk_s[:], in_=bqk_cols), writes=["bqk_s"])
    T.dma("sync", "const", lambda e: e.dma_start(out=bv_f[:], in_=bv_cols), writes=["bv_f"])
    T.dma("sync", "const", lambda e: e.dma_start(out=bproj_s[:], in_=bproj_row), writes=["bproj_s"])
    T.dma("sync", "const", lambda e: e.dma_start(out=sinsg[0:64, :], in_=sinT), writes=["sinsg"])
    T.dma("sync", "const", lambda e: e.dma_start(out=sinsg[64:128, :], in_=sinT), reads=["sinsg"], writes=["sinsg"])
    T.dma("sync", "const", lambda e: e.dma_start(out=cos2[0:64, :], in_=cosT), writes=["cos2a"])
    T.dma("sync", "const", lambda e: e.dma_start(out=cos2[64:128, :], in_=cosT), writes=["cos2b"])
    T.g(lambda e: e.memset(ones_bf[:], 1.0), writes=["ones_bf"])
    T.g(lambda e: e.memset(ones1_f[:], 1.0), writes=["ones1_f"])
    T.g(lambda e: e.memset(ones1_bf[:], 1.0), writes=["ones1_bf"])
    T.g(lambda e: e.memset(konst[:, 0:1], 1e-5), writes=["konst"])
    T.g(lambda e: e.memset(konst[:, 1:2], EXPSHIFT), reads=["konst"], writes=["konst"])
    for j in range(8):
        T.dma("gpsimd", "wp", lambda e, j=j: e.dma_start(
            out=wp_bf[:, j * D:(j + 1) * D], in_=w_proj[j * 128:(j + 1) * 128, :]),
            writes=[f"wp{j}"])
    T.v(lambda e: e.tensor_copy(out=bv_bf[:], in_=bv_f[:]), reads=["bv_f"], writes=["bv_bf"])
    for qtr in (0, 2):
        lo = qtr * 32
        T.v(lambda e, lo=lo: e.tensor_scalar_mul(
            out=sinsg[lo:lo + 32, :], in0=sinsg[lo:lo + 32, :], scalar1=-1.0),
            reads=["sinsg"], writes=["sinsg"])

    cstack = _ctx.ExitStack()
    psc = [cstack.enter_context(nc.psum_tensor(f"l0a_psc{i}", [128, 512], F32)) for i in range(2)]
    for j in range(8):
        for half in range(2):
            T.t(lambda e, j=j, h=half: e.matmul(
                out=psc[h][0:1, 0:512], lhsT=bv_bf[:, j:j + 1],
                rhs=wp_bf[:, j * D + h * 512: j * D + (h + 1) * 512],
                start=(j == 0), stop=(j == 7)),
                reads=["bv_bf", f"wp{j}"], writes=[f"psc{half}"])
    T.v(lambda e: e.tensor_add(crow_f[0:1, 0:512], psc[0][0:1, 0:512], bproj_s[0:1, 0:512]),
        reads=["psc0", "bproj_s"], writes=["crow_f"])
    T.v(lambda e: e.tensor_add(crow_f[0:1, 512:1024], psc[1][0:1, 0:512], bproj_s[0:1, 512:1024]),
        reads=["psc1", "bproj_s"], writes=["crow_f"])
    T.v(lambda e: e.tensor_copy(out=crow_bf[:], in_=crow_f[:]), reads=["crow_f"], writes=["crow_bf"])
    T.barrier()
    cstack.close()

    for b in range(NB):
        base = b * TB
        lstack = _ctx.ExitStack()
        ps = [lstack.enter_context(nc.psum_tensor(f"l0a_psl{b}{i}", [128, 512], F32)) for i in range(8)]
        for j in range(8):
            r = j % 2
            T.dma("sync", f"lnx{r}", lambda e, j=j, r=r, base=base: e.dma_start(
                out=xrot[r][:], in_=xT[j * 128:(j + 1) * 128, base:base + TB]),
                reads=[f"xrot{r}"], writes=[f"xrot{r}"], tag=f"lnld{b}{j}")
            for ncc in range(4):
                rr = (j * 4 + ncc) % 2
                T.v(lambda e, r=r, rr=rr, ncc=ncc: e.tensor_copy(
                    out=xbrot[rr][:], in_=xrot[r][:, ncc * QC:(ncc + 1) * QC]),
                    reads=[f"xrot{r}", f"xb{rr}"], writes=[f"xb{rr}"])
                T.s(lambda e, r=r, rr=rr, ncc=ncc: e.activation(
                    out=xqrot[rr][:], in_=xrot[r][:, ncc * QC:(ncc + 1) * QC], func=AF.Square),
                    reads=[f"xrot{r}", f"xq{rr}"], writes=[f"xq{rr}"])
                T.t(lambda e, j=j, rr=rr, ncc=ncc: e.matmul(
                    out=ps[ncc][:], lhsT=ones_bf[:], rhs=xbrot[rr][:],
                    start=(j == 0), stop=(j == 7)),
                    reads=["ones_bf", f"xb{rr}"], writes=[f"ps{ncc}"])
                T.t(lambda e, j=j, rr=rr, ncc=ncc: e.matmul(
                    out=ps[4 + ncc][:], lhsT=ones_bf[:], rhs=xqrot[rr][:],
                    start=(j == 0), stop=(j == 7)),
                    reads=["ones_bf", f"xq{rr}"], writes=[f"ps{4 + ncc}"])
        for ncc in range(4):
            sl = slice(ncc * QC, (ncc + 1) * QC)
            T.s(lambda e, sl=sl, ncc=ncc: e.activation(
                out=mean_s[:, sl], in_=ps[ncc][:], func=AF.Identity, scale=1.0 / D),
                reads=[f"ps{ncc}"], writes=["mean_s"])
            T.s(lambda e, sl=sl, ncc=ncc: e.activation(
                out=rstd_s[:, sl], in_=ps[4 + ncc][:], func=AF.Identity, scale=1.0 / D),
                reads=[f"ps{4 + ncc}"], writes=["rstd_s"])
            T.v(lambda e, sl=sl: e.tensor_mul(stat_t[:], mean_s[:, sl], mean_s[:, sl]),
                reads=["mean_s"], writes=["stat_t"])
            T.v(lambda e, sl=sl: e.tensor_sub(rstd_s[:, sl], rstd_s[:, sl], stat_t[:]),
                reads=["stat_t", "rstd_s"], writes=["rstd_s"])
            T.s(lambda e, sl=sl: e.activation(out=rstd_s[:, sl], in_=rstd_s[:, sl], func=AF.Ln, bias=konst[:, 0:1]),
                reads=["rstd_s", "konst"], writes=["rstd_s"])
            T.s(lambda e, sl=sl: e.activation(out=rstd_s[:, sl], in_=rstd_s[:, sl], func=AF.Exp, scale=-0.5),
                reads=["rstd_s"], writes=["rstd_s"])
        for j in range(8):
            r = j % 2
            T.dma("sync", f"lnx{r}", lambda e, j=j, r=r, base=base: e.dma_start(
                out=xrot[r][:], in_=xT[j * 128:(j + 1) * 128, base:base + TB]),
                reads=[f"xrot{r}"], writes=[f"xrot{r}"])
            T.v(lambda e, r=r: e.tensor_sub(qraw[:], xrot[r][:], mean_s[:]),
                reads=[f"xrot{r}", "mean_s", "qraw"], writes=["qraw"])
            T.v(lambda e, j=j: e.tensor_mul(xn_b[:, j * TB:(j + 1) * TB], qraw[:], rstd_s[:]),
                reads=["qraw", "rstd_s"], writes=[f"xn{j}"])

        T.barrier()
        lstack.close()
        astack = _ctx.ExitStack()
        psS = [astack.enter_context(nc.psum_tensor(f"l0a_psS{b}{i}", [128, 1024], F32)) for i in range(2)]
        psq = [astack.enter_context(nc.psum_tensor(f"l0a_psq{b}{i}", [128, 512], F32)) for i in range(2)]
        psPV = [astack.enter_context(nc.psum_tensor(f"l0a_psPV{b}{i}", [128, 512], F32)) for i in range(2)]
        T.g(lambda e: e.memset(vsl[:].rearrange("p (t g) -> p t g", t=16)[:, :, 64:65], 1.0),
            reads=["vsl"], writes=["vsl"], tag="vones")
        T.g(lambda e: e.memset(vsl[:].rearrange("p (t g) -> p t g", t=16)[:, :, 129:130], 1.0),
            reads=["vsl"], writes=["vsl"])

        for p in range(NPAIR):
            for which, buf, name in ((0, wqc, "wqc"), (1, wkc, "wkc"), (2, wvc, "wvc")):
                col0 = which * D + p * 128
                T.dma("gpsimd", f"wpair{which}", lambda e, buf=buf, col0=col0: e.dma_start(
                    out=buf[:].rearrange("p (j n) -> p j n", j=8),
                    in_=w_qkv.rearrange("(j p) n -> p j n", j=8)[:, :, col0:col0 + 128]),
                    reads=[name], writes=[name], tag=f"wld{b}{p}{which}")
            for which, wbuf, wname, slab, slabname in (
                    (0, wqc, "wqc", qsl, "qsl"), (1, wkc, "wkc", ksl, "ksl")):
                for ncc in range(4):
                    bank = ncc % 2
                    for j in range(8):
                        T.t(lambda e, wbuf=wbuf, j=j, ncc=ncc, bank=bank: e.matmul(
                            out=psq[bank][:], lhsT=wbuf[:, j * 128:(j + 1) * 128],
                            rhs=xn_b[:, j * TB + ncc * QC: j * TB + (ncc + 1) * QC],
                            start=(j == 0), stop=(j == 7)),
                            reads=[wname, f"xn{j}"], writes=[f"psq{bank}"])
                    ft = which * 8 + p
                    T.s(lambda e, ncc=ncc, bank=bank, ft=ft: e.activation(
                        out=qraw[:, ncc * QC:(ncc + 1) * QC], in_=psq[bank][:],
                        func=AF.Identity, bias=bqk_s[:, ft:ft + 1]),
                        reads=[f"psq{bank}", "bqk_s", "qraw"], writes=["qraw"])
                T.v(lambda e, slab=slab: e.tensor_copy(out=slab[:, 0:PRE], in_=qraw[:, 0:PRE]),
                    reads=["qraw"], writes=[slabname])
                T.v(lambda e: e.tensor_mul(rtmp[:], qraw[:, PRE:TB], cos2[:]),
                    reads=["qraw", "cos2a", "cos2b", "rtmp"], writes=["rtmp"])
                for qtr in range(4):
                    lo = qtr * 32
                    olo = (qtr + 1) * 32 if qtr % 2 == 0 else (qtr - 1) * 32
                    T.v(lambda e, lo=lo, olo=olo: e.tensor_copy(
                        out=rrot[lo:lo + 32, :], in_=qraw[olo:olo + 32, PRE:TB]),
                        reads=["qraw", "rrot"], writes=["rrot"])
                T.v(lambda e: e.tensor_mul(rrot[:], rrot[:], sinsg[:]),
                    reads=["rrot", "sinsg"], writes=["rrot"])
                T.v(lambda e, slab=slab: e.tensor_add(slab[:, PRE:TB], rtmp[:], rrot[:]),
                    reads=["rtmp", "rrot"], writes=[slabname])
            for tt in range(16):
                bank = tt % 2
                for j in range(8):
                    T.t(lambda e, tt=tt, j=j, bank=bank: e.matmul(
                        out=psq[bank][:, 0:128], lhsT=xn_b[:, j * TB + tt * 128: j * TB + (tt + 1) * 128],
                        rhs=wvc[:, j * 128:(j + 1) * 128],
                        start=(j == 0), stop=(j == 7)),
                        reads=["wvc", f"xn{j}"], writes=[f"psq{bank}"])
                T.s(lambda e, tt=tt, bank=bank: e.copy(
                    out=vsl[:, tt * 130: tt * 130 + 64], in_=psq[bank][:, 0:64]),
                    reads=[f"psq{bank}"], writes=["vsl"], tag=f"vev{tt}")
                T.s(lambda e, tt=tt, bank=bank: e.copy(
                    out=vsl[:, tt * 130 + 65: tt * 130 + 129], in_=psq[bank][:, 64:128]),
                    reads=[f"psq{bank}"], writes=["vsl"])
            for qc in range(NQC):
                qsll = slice(qc * QC, (qc + 1) * QC)

                def emit_qk(kc, par):
                    T.t(lambda e, kc=kc, par=par, qsll=qsll: e.matmul(
                        out=psS[par][:, 0:512], lhsT=ksl[0:64, kc * 128:(kc + 1) * 128],
                        rhs=qsl[0:64, qsll], start=True, stop=True, tile_position=(0, 0)),
                        reads=["ksl", "qsl"], writes=[f"psS{par}"])
                    T.t(lambda e, kc=kc, par=par, qsll=qsll: e.matmul(
                        out=psS[par][:, 512:1024], lhsT=ksl[64:128, kc * 128:(kc + 1) * 128],
                        rhs=qsl[64:128, qsll], start=True, stop=True, tile_position=(64, 0)),
                        reads=["ksl", "qsl"], writes=[f"psS{par}"])

                def emit_exp(kc, par):
                    T.s(lambda e, par=par: e.activation(
                        out=ebuf[:, par * 1024:(par + 1) * 1024], in_=psS[par][:],
                        func=AF.Exp, scale=0.125, bias=konst[:, 1:2]),
                        reads=[f"psS{par}", "konst"], writes=[f"eb{par}"])

                def emit_pv(kc, par):
                    T.t(lambda e, kc=kc, par=par: e.matmul(
                        out=psPV[0][0:65, :], lhsT=vsl[:, kc * 130: kc * 130 + 65],
                        rhs=ebuf[:, par * 1024: par * 1024 + 512],
                        start=(kc == 0), stop=(kc == NKC - 1)),
                        reads=["vsl", f"eb{par}"], writes=["psPV0"])
                    T.t(lambda e, kc=kc, par=par: e.matmul(
                        out=psPV[1][0:65, :], lhsT=vsl[:, kc * 130 + 65: kc * 130 + 130],
                        rhs=ebuf[:, par * 1024 + 512: par * 1024 + 1024],
                        start=(kc == 0), stop=(kc == NKC - 1)),
                        reads=["vsl", f"eb{par}"], writes=["psPV1"])

                for kc in range(NKC):
                    emit_qk(kc, kc % 2)
                    if kc >= 1:
                        emit_pv(kc - 1, (kc - 1) % 2)
                    emit_exp(kc, kc % 2)
                emit_pv(NKC - 1, (NKC - 1) % 2)
                for head in range(2):
                    T.v(lambda e, head=head: e.reciprocal(out=rrow[:], in_=psPV[head][64:65, :]),
                        reads=[f"psPV{head}"], writes=["rrow"])
                    T.t(lambda e: e.matmul(out=psq[0][0:64, :], lhsT=ones1_f[:], rhs=rrow[:],
                                           start=True, stop=True),
                        reads=["rrow", "ones1_f"], writes=["psq0"])
                    T.s(lambda e, head=head: e.copy(out=rrb_s[:, head * QC:(head + 1) * QC],
                                                    in_=psq[0][0:64, :]),
                        reads=["psq0"], writes=[f"rrb{head}"])
                    T.v(lambda e, p=p, head=head, qc=qc: e.tensor_mul(
                        pvdiv[head * 64:(head + 1) * 64, p * TB + qc * QC: p * TB + (qc + 1) * QC],
                        psPV[head][0:64, :], rrb_s[:, head * QC:(head + 1) * QC]),
                        reads=[f"psPV{head}", f"rrb{head}"], writes=[f"pv{p}"])

        for e_t in range(8):
            r = e_t % 2
            T.dma("sync", f"prx{r}", lambda e, e_t=e_t, r=r, base=base: e.dma_start(
                out=xrot[r][:], in_=xT[e_t * 128:(e_t + 1) * 128, base:base + TB]),
                reads=[f"xrot{r}"], writes=[f"xrot{r}"])
            for qc in range(NQC):
                bank = qc % 2
                for j in range(8):
                    T.t(lambda e, e_t=e_t, j=j, qc=qc, bank=bank: e.matmul(
                        out=psq[bank][:], lhsT=wp_bf[:, j * D + e_t * 128: j * D + (e_t + 1) * 128],
                        rhs=pvdiv[:, j * TB + qc * QC: j * TB + (qc + 1) * QC],
                        start=(j == 0), stop=False),
                        reads=[f"wp{j}", f"pv{j}"], writes=[f"psq{bank}"])
                T.t(lambda e, e_t=e_t, qc=qc, bank=bank: e.matmul(
                    out=psq[bank][:], lhsT=crow_bf[0:1, e_t * 128:(e_t + 1) * 128],
                    rhs=ones1_bf[:], start=False, stop=True),
                    reads=["crow_bf", "ones1_bf"], writes=[f"psq{bank}"])
                T.v(lambda e, qc=qc, bank=bank, r=r: e.tensor_add(
                    xrot[r][:, qc * QC:(qc + 1) * QC], psq[bank][:],
                    xrot[r][:, qc * QC:(qc + 1) * QC]),
                    reads=[f"psq{bank}", f"xrot{r}"], writes=[f"xrot{r}"])
            T.dma("sync", f"out{r}", lambda e, e_t=e_t, r=r, base=base: e.dma_start(
                out=x1T[e_t * 128:(e_t + 1) * 128, base:base + TB], in_=xrot[r][:]),
                reads=[f"xrot{r}"], writes=[f"x1out{e_t}"], tag=f"store{b}{e_t}")
        T.barrier()
        astack.close()

    bstack.close()
    return


def prep_l0a(x_core, ins):
    d = {}
    d["xT"] = np.ascontiguousarray(x_core.transpose(2, 0, 1).reshape(D, NB * TB))
    d["w_qkv"] = np.ascontiguousarray(ins["L0_attn_w_qkv"])
    b_qkv = ins["L0_attn_b_qkv"]
    d["bqk_cols"] = np.ascontiguousarray(b_qkv[:2048].reshape(16, 128).T)
    d["bv_cols"] = np.ascontiguousarray(b_qkv[2048:].reshape(8, 128).T)
    d["bproj_row"] = np.ascontiguousarray(ins["L0_attn_b_proj"].reshape(1, D))
    d["w_proj"] = np.ascontiguousarray(ins["L0_attn_w_proj"])
    d["sinT"] = np.ascontiguousarray(ins["L0_attn_sin"].T)
    d["cosT"] = np.ascontiguousarray(ins["L0_attn_cos"].T)
    return d


F32 = mybir.dt.float32
BF16 = mybir.dt.bfloat16
AF = mybir.ActivationFunctionType
ALU = mybir.AluOpType

D = 1024
HID = 4096
TB = 2048
NB = 2
QC = 512
NQC = TB // QC
MAGIC = float(np.float32(1.5 * 2 ** 23))


def emit_l0ffn(nc, T, ps, ins, x_in, x_out, n_cores=8):
    import contextlib as _ctx
    bstack = _ctx.ExitStack()
    ent = bstack.enter_context
    x1T = x_in
    x2T = x_out
    W1T = ins['W1T']
    W2T = ins['W2T']
    b1_cols = ins['b1_cols']
    b2_cols = ins['b2_cols']
    ident_in = ins['ident_f']

    w1q_d = nc.dram_tensor("l0f_w1q_d", [D, HID], BF16, kind="Internal")
    w2q_d = nc.dram_tensor("l0f_w2q_d", [HID, D], BF16, kind="Internal")
    xn_d = nc.dram_tensor("l0f_xn_d", [D, NB * TB], F32, kind="Internal").ap()
    g2_d = nc.dram_tensor("l0f_g2_d", [HID, NB * TB], F32, kind="Internal").ap()
    mx_d = nc.dram_tensor("l0f_mx_d", [1, TB], F32, kind="Internal")
    ar_d = nc.dram_tensor("l0f_ar_d", [1, TB], F32, kind="Internal")

    sb = lambda name, shape, dt: ent(nc.sbuf_tensor('l0f_' + name, shape, dt))

    xqb = sb("xqb", [128, 8 * TB], BF16)
    w1g = [sb(f"w1g{i}", [128, 8 * 512], BF16) for i in range(2)]
    w2s = [sb(f"w2s{i}", [128, 32 * 128], BF16) for i in range(2)]
    xrot = [sb(f"xrot{i}", [128, TB], F32) for i in range(2)]
    stat_t = sb("stat_t", [128, QC], F32)
    stat2 = sb("stat2", [128, QC], F32)
    import contextlib as _ctx
    maxacc = sb("maxacc", [128, TB], F32)
    trcol = sb("trcol", [128, 16], F32)
    mrow = sb("mrow", [1, TB], F32)
    alpha_b = sb("alpha_b", [128, TB], F32)
    gsc_b = sb("gsc_b", [128, TB], F32)
    tq = [sb(f"tq{i}", [128, TB], F32) for i in range(2)]
    wsum = sb("wsum", [128, 2], F32)
    wtot = sb("wtot", [1, 2], F32)
    wcol = sb("wcol", [128, 2], F32)
    wscol = sb("wscol", [128, 2], F32)
    b1s = sb("b1s", [128, 32], F32)
    b2s = sb("b2s", [128, 8], F32)
    ident_f = sb("identf", [128, 128], F32)
    ones_bf = sb("ones_bf", [128, 128], BF16)
    ones_f = sb("ones_f", [128, 1], F32)
    ones1_f = sb("ones1_f", [1, 128], F32)
    konst = sb("konst", [128, 1], F32)
    wstack = _ctx.ExitStack()
    wrot = [wstack.enter_context(nc.sbuf_tensor(f"l0f_wrot{i}", [128, TB], F32)) for i in range(2)]
    ps = [ent(nc.psum_tensor(f"l0f_ps{i}", [128, 512], F32)) for i in range(8)]


    T.dma("sync", "const", lambda e: e.dma_start(out=b1s[:], in_=b1_cols), writes=["b1s"])
    T.dma("sync", "const", lambda e: e.dma_start(out=b2s[:], in_=b2_cols), writes=["b2s"])
    T.dma("sync", "const", lambda e: e.dma_start(out=ident_f[:], in_=ident_in), writes=["identf"])
    T.g(lambda e: e.memset(ones_bf[:], 1.0), writes=["ones_bf"])
    T.g(lambda e: e.memset(ones_f[:], 1.0), writes=["ones_f"])
    T.g(lambda e: e.memset(ones1_f[:], 1.0), writes=["ones1_f"])
    T.g(lambda e: e.memset(konst[:], 1e-5), writes=["konst"])

    for wi, (wdram, qdram, nchunk, cw) in enumerate(
            ((W1T, w1q_d, 8, HID), (W2T, w2q_d, 32, D))):
        nhalf = cw // TB if cw >= TB else 1
        hw = cw // nhalf
        first = True
        for j in range(nchunk):
            for hh in range(nhalf):
                r = (j * nhalf + hh) % 2
                T.dma("sync", f"wld{r}", lambda e, wdram=wdram, j=j, hh=hh, hw=hw, r=r: e.dma_start(
                    out=wrot[r][:, 0:hw], in_=wdram[j * 128:(j + 1) * 128, hh * hw:(hh + 1) * hw]),
                    reads=[f"wrot{r}"], writes=[f"wrot{r}"])
                T.v(lambda e, r=r, hw=hw: e.tensor_reduce(
                    out=stat_t[:, 0:1], in_=wrot[r][:, 0:hw], axis=mybir.AxisListType.X,
                    op=ALU.add, apply_absolute_value=True),
                    reads=[f"wrot{r}", "stat_t"], writes=["stat_t"])
                if first:
                    T.v(lambda e, wi=wi: e.tensor_copy(out=wsum[:, wi:wi + 1], in_=stat_t[:, 0:1]),
                        reads=["stat_t"], writes=["wsum"])
                    first = False
                else:
                    T.v(lambda e, wi=wi: e.tensor_add(wsum[:, wi:wi + 1], wsum[:, wi:wi + 1],
                                                      stat_t[:, 0:1]),
                        reads=["stat_t", "wsum"], writes=["wsum"])
        nel = float(nchunk * 128 * cw)
        T.t(lambda e, wi=wi: e.matmul(out=ps[7][0:1, 0:1], lhsT=wsum[:, wi:wi + 1], rhs=ones_f[:],
                                      start=True, stop=True),
            reads=["wsum", "ones_f"], writes=["ps7"])
        T.s(lambda e, wi=wi, nel=nel: e.activation(out=wtot[0:1, wi:wi + 1], in_=ps[7][0:1, 0:1],
                                                   func=AF.Identity, scale=1.0 / nel),
            reads=["ps7", "wtot"], writes=["wtot"])
        T.v(lambda e, wi=wi: e.tensor_scalar_max(out=wtot[0:1, wi:wi + 1], in0=wtot[0:1, wi:wi + 1],
                                                 scalar1=1e-5),
            reads=["wtot"], writes=["wtot"])
        T.t(lambda e, wi=wi: e.matmul(out=ps[7][:, 0:1], lhsT=ones1_f[:], rhs=wtot[0:1, wi:wi + 1],
                                      start=True, stop=True),
            reads=["wtot", "ones1_f"], writes=["ps7"])
        T.s(lambda e, wi=wi: e.copy(out=wscol[:, wi:wi + 1], in_=ps[7][:, 0:1]),
            reads=["ps7", "wscol"], writes=["wscol"])
        T.v(lambda e, wi=wi: e.reciprocal(out=wcol[:, wi:wi + 1], in_=ps[7][:, 0:1]),
            reads=["ps7", "wcol"], writes=["wcol"])
        for j in range(nchunk):
            for hh in range(nhalf):
                r = (j * nhalf + hh) % 2
                T.dma("sync", f"wld{r}", lambda e, wdram=wdram, j=j, hh=hh, hw=hw, r=r: e.dma_start(
                    out=wrot[r][:, 0:hw], in_=wdram[j * 128:(j + 1) * 128, hh * hw:(hh + 1) * hw]),
                    reads=[f"wrot{r}"], writes=[f"wrot{r}"])
                T.v(lambda e, r=r, wi=wi, hw=hw: e.tensor_scalar_mul(
                    out=wrot[r][:, 0:hw], in0=wrot[r][:, 0:hw], scalar1=wcol[:, wi:wi + 1]),
                    reads=[f"wrot{r}", "wcol"], writes=[f"wrot{r}"])
                T.v(lambda e, r=r, hw=hw: e.tensor_scalar(
                    out=wrot[r][:, 0:hw], in0=wrot[r][:, 0:hw], scalar1=MAGIC, scalar2=MAGIC,
                    op0=ALU.add, op1=ALU.subtract),
                    reads=[f"wrot{r}"], writes=[f"wrot{r}"])
                T.v(lambda e, r=r, hw=hw: e.tensor_scalar(
                    out=xqb[:, 0:hw], in0=wrot[r][:, 0:hw], scalar1=1.0, scalar2=-1.0,
                    op0=ALU.min, op1=ALU.max),
                    reads=[f"wrot{r}", "xqb"], writes=["xqb"])
                T.dma("sync", f"wqout{r}", lambda e, qdram=qdram, j=j, hh=hh, hw=hw: e.dma_start(
                    out=qdram.ap()[j * 128:(j + 1) * 128, hh * hw:(hh + 1) * hw], in_=xqb[:, 0:hw]),
                    reads=["xqb"], writes=[f"wq{wi}"], tag=f"wqo{wi}{j}{hh}")

    T.barrier()
    wstack.close()

    def alpha_finalize(wi_ws):
        T.v(lambda e: e.tensor_copy(out=tq[0][0:64, :], in_=maxacc[64:128, :]),
            reads=["maxacc", "tq0"], writes=["tq0"])
        T.v(lambda e: e.tensor_max(maxacc[0:64, :], maxacc[0:64, :], tq[0][0:64, :]),
            reads=["tq0", "maxacc"], writes=["maxacc"])
        T.v(lambda e: e.tensor_copy(out=tq[0][0:32, :], in_=maxacc[32:64, :]),
            reads=["maxacc", "tq0"], writes=["tq0"])
        T.v(lambda e: e.tensor_max(tq[1][0:32, :], maxacc[0:32, :], tq[0][0:32, :]),
            reads=["tq0", "maxacc", "tq1"], writes=["tq1"])
        for tt in range(16):
            T.t(lambda e, tt=tt: e.transpose(
                out=ps[6][:, (tt % 4) * 32:(tt % 4) * 32 + 32],
                in_=tq[1][0:32, tt * 128:(tt + 1) * 128], identity=ident_f[0:32, 0:32]),
                reads=["tq1", "identf"], writes=["ps6"])
            T.v(lambda e, tt=tt: e.tensor_reduce(
                out=trcol[:, tt:tt + 1], in_=ps[6][:, (tt % 4) * 32:(tt % 4) * 32 + 32],
                axis=mybir.AxisListType.X, op=ALU.max),
                reads=["ps6", "trcol"], writes=["trcol"])
        for tt in range(16):
            T.dma("sync", "mxo", lambda e, tt=tt: e.dma_start(
                out=AP(mx_d, tt * 128, [[1, 128], [1, 1]]), in_=trcol[:, tt:tt + 1]),
                reads=["trcol"], writes=["mx_d"])
        T.dma("sync", "mxi", lambda e: e.dma_start(out=mrow[:], in_=mx_d.ap()),
            reads=["mx_d", "mrow"], writes=["mrow"])
        T.v(lambda e: e.tensor_scalar_max(out=mrow[:], in0=mrow[:], scalar1=1e-5),
            reads=["mrow"], writes=["mrow"])
        T.v(lambda e: e.reciprocal(out=mrow[:], in_=mrow[:]), reads=["mrow"], writes=["mrow"])
        T.v(lambda e: e.tensor_scalar_mul(out=mrow[:], in0=mrow[:], scalar1=127.0),
            reads=["mrow"], writes=["mrow"])
        T.dma("sync", "aro", lambda e: e.dma_start(out=ar_d.ap(), in_=mrow[:]),
            reads=["mrow"], writes=["ar_d"])
        T.dma("sync", "ari", lambda e: e.dma_start(
            out=alpha_b[:], in_=AP(ar_d, 0, [[0, 128], [1, TB]])),
            reads=["ar_d", "alpha_b"], writes=["alpha_b"])
        T.v(lambda e: e.reciprocal(out=gsc_b[:], in_=alpha_b[:]), reads=["alpha_b", "gsc_b"], writes=["gsc_b"])
        T.v(lambda e, wi_ws=wi_ws: e.tensor_scalar_mul(out=gsc_b[:], in0=gsc_b[:],
                                                       scalar1=wscol[:, wi_ws:wi_ws + 1]),
            reads=["gsc_b", "wscol"], writes=["gsc_b"])

    for b in range(NB):
        base = b * TB
        lnstack = _ctx.ExitStack()
        _l = lnstack.enter_context
        mean_s = _l(nc.sbuf_tensor(f"l0f_mean{b}", [128, TB], F32))
        rstd_s = _l(nc.sbuf_tensor(f"l0f_rstd{b}", [128, TB], F32))
        xbrot = [_l(nc.sbuf_tensor(f"l0f_xbrot{b}{i}", [128, QC], BF16)) for i in range(2)]
        xqrot = [_l(nc.sbuf_tensor(f"l0f_xqrot{b}{i}", [128, QC], BF16)) for i in range(2)]
        for j in range(8):
            r = j % 2
            T.dma("sync", f"lnx{r}", lambda e, j=j, r=r, base=base: e.dma_start(
                out=xrot[r][:], in_=x1T[j * 128:(j + 1) * 128, base:base + TB]),
                reads=[f"xrot{r}"], writes=[f"xrot{r}"])
            for ncc in range(4):
                rr = (j * 4 + ncc) % 2
                T.v(lambda e, r=r, rr=rr, ncc=ncc: e.tensor_copy(
                    out=xbrot[rr][:], in_=xrot[r][:, ncc * QC:(ncc + 1) * QC]),
                    reads=[f"xrot{r}", f"xb{rr}"], writes=[f"xb{rr}"])
                T.s(lambda e, r=r, rr=rr, ncc=ncc: e.activation(
                    out=xqrot[rr][:], in_=xrot[r][:, ncc * QC:(ncc + 1) * QC], func=AF.Square),
                    reads=[f"xrot{r}", f"xq{rr}"], writes=[f"xq{rr}"])
                T.t(lambda e, j=j, rr=rr, ncc=ncc: e.matmul(
                    out=ps[ncc][:], lhsT=ones_bf[:], rhs=xbrot[rr][:],
                    start=(j == 0), stop=(j == 7)),
                    reads=["ones_bf", f"xb{rr}"], writes=[f"ps{ncc}"])
                T.t(lambda e, j=j, rr=rr, ncc=ncc: e.matmul(
                    out=ps[4 + ncc][:], lhsT=ones_bf[:], rhs=xqrot[rr][:],
                    start=(j == 0), stop=(j == 7)),
                    reads=["ones_bf", f"xq{rr}"], writes=[f"ps{4 + ncc}"])
        for ncc in range(4):
            sl = slice(ncc * QC, (ncc + 1) * QC)
            T.s(lambda e, sl=sl, ncc=ncc: e.activation(
                out=mean_s[:, sl], in_=ps[ncc][:], func=AF.Identity, scale=1.0 / D),
                reads=[f"ps{ncc}", "mean_s"], writes=["mean_s"])
            T.s(lambda e, sl=sl, ncc=ncc: e.activation(
                out=rstd_s[:, sl], in_=ps[4 + ncc][:], func=AF.Identity, scale=1.0 / D),
                reads=[f"ps{4 + ncc}", "rstd_s"], writes=["rstd_s"])
            T.v(lambda e, sl=sl: e.tensor_mul(stat_t[:], mean_s[:, sl], mean_s[:, sl]),
                reads=["mean_s", "stat_t"], writes=["stat_t"])
            T.v(lambda e, sl=sl: e.tensor_sub(rstd_s[:, sl], rstd_s[:, sl], stat_t[:]),
                reads=["stat_t", "rstd_s"], writes=["rstd_s"])
            T.s(lambda e, sl=sl: e.activation(out=rstd_s[:, sl], in_=rstd_s[:, sl], func=AF.Ln,
                                              bias=konst[:, 0:1]),
                reads=["rstd_s", "konst"], writes=["rstd_s"])
            T.s(lambda e, sl=sl: e.activation(out=rstd_s[:, sl], in_=rstd_s[:, sl], func=AF.Exp,
                                              scale=-0.5),
                reads=["rstd_s"], writes=["rstd_s"])
        for j in range(8):
            r = j % 2
            T.dma("sync", f"lnx{r}", lambda e, j=j, r=r, base=base: e.dma_start(
                out=xrot[r][:], in_=x1T[j * 128:(j + 1) * 128, base:base + TB]),
                reads=[f"xrot{r}"], writes=[f"xrot{r}"])
            T.v(lambda e, r=r: e.tensor_sub(tq[0][:], xrot[r][:], mean_s[:]),
                reads=[f"xrot{r}", "mean_s", "tq0"], writes=["tq0"])
            T.v(lambda e: e.tensor_mul(tq[1][:], tq[0][:], rstd_s[:]),
                reads=["tq0", "rstd_s", "tq1"], writes=["tq1"])
            T.dma("sync", f"xnout{r}", lambda e, j=j, base=base: e.dma_start(
                out=xn_d[j * 128:(j + 1) * 128, base:base + TB], in_=tq[1][:]),
                reads=["tq1"], writes=[f"xnd{j}"])
            T.s(lambda e: e.activation(out=tq[0][:], in_=tq[1][:], func=AF.Abs),
                reads=["tq1", "tq0"], writes=["tq0"])
            if j == 0:
                T.v(lambda e: e.tensor_copy(out=maxacc[:], in_=tq[0][:]),
                    reads=["tq0", "maxacc"], writes=["maxacc"])
            else:
                T.v(lambda e: e.tensor_max(maxacc[:], maxacc[:], tq[0][:]),
                    reads=["tq0", "maxacc"], writes=["maxacc"])
        T.barrier()
        lnstack.close()
        alpha_finalize(0)
        for j in range(8):
            r = j % 2
            T.dma("sync", f"xnin{r}", lambda e, j=j, base=base, r=r: e.dma_start(
                out=tq[r][:], in_=xn_d[j * 128:(j + 1) * 128, base:base + TB]),
                reads=[f"tq{r}", f"xnd{j}"], writes=[f"tq{r}"])
            T.v(lambda e, r=r: e.tensor_mul(tq[r][:], tq[r][:], alpha_b[:]),
                reads=[f"tq{r}", "alpha_b"], writes=[f"tq{r}"])
            T.v(lambda e, r=r, j=j: e.tensor_scalar(
                out=xqb[:, j * TB:(j + 1) * TB], in0=tq[r][:], scalar1=MAGIC, scalar2=MAGIC,
                op0=ALU.add, op1=ALU.subtract),
                reads=[f"tq{r}", "xqb"], writes=["xqb"])
        mmstack = _ctx.ExitStack()
        _m = mmstack.enter_context
        g2w = [_m(nc.sbuf_tensor(f"l0f_g2w{b}{i}", [128, 4 * QC], F32)) for i in range(2)]
        hqcp = [_m(nc.sbuf_tensor(f"l0f_hqcp{b}", [128, 32 * QC], BF16))] * 2
        for hg in range(8):
            W = hg % 2
            T.dma("sync", f"w1l{W}", lambda e, hg=hg, W=W: e.dma_start(
                out=w1g[W][:].rearrange("p (j n) -> p j n", j=8),
                in_=w1q_d.ap().rearrange("(j p) n -> p j n", j=8)[:, :, hg * 512:(hg + 1) * 512]),
                reads=[f"w1g{W}"], writes=[f"w1g{W}"])
            for qc in range(NQC):
                G = qc % 2
                for hi in range(4):
                    ht = hg * 4 + hi
                    bank = G * 4 + hi
                    for j in range(8):
                        T.t(lambda e, W=W, j=j, qc=qc, bank=bank, hi=hi: e.matmul(
                            out=ps[bank][:], lhsT=w1g[W][:, j * 512 + hi * 128: j * 512 + (hi + 1) * 128],
                            rhs=xqb[:, j * TB + qc * QC: j * TB + (qc + 1) * QC],
                            start=(j == 0), stop=(j == 7)),
                            reads=[f"w1g{W}", "xqb"], writes=[f"ps{bank}"])
                    T.v(lambda e, qc=qc, bank=bank, G=G, hi=hi: e.tensor_mul(
                        g2w[G][:, hi * QC:(hi + 1) * QC], ps[bank][:],
                        gsc_b[:, qc * QC:(qc + 1) * QC]),
                        reads=[f"ps{bank}", "gsc_b", f"g2w{G}"], writes=[f"g2w{G}"])
                for hi in range(4):
                    ht = hg * 4 + hi
                    T.s(lambda e, ht=ht, G=G, hi=hi: e.activation(
                        out=g2w[G][:, hi * QC:(hi + 1) * QC], in_=g2w[G][:, hi * QC:(hi + 1) * QC],
                        func=AF.Gelu, bias=b1s[:, ht:ht + 1]),
                        reads=[f"g2w{G}", "b1s"], writes=[f"g2w{G}"])
                T.dma("sync", f"g2o{G}", lambda e, hg=hg, qc=qc, base=base, G=G: e.dma_start(
                    out=g2_d.rearrange("(h p) n -> p h n", h=32)[:, hg * 4:(hg + 1) * 4,
                                                                base + qc * QC: base + (qc + 1) * QC],
                    in_=g2w[G][:].rearrange("p (h n) -> p h n", h=4)),
                    reads=[f"g2w{G}"], writes=["g2_d+"])
                for hi in range(4):
                    T.s(lambda e, G=G, hi=hi: e.activation(out=stat2[:], in_=g2w[G][:, hi * QC:(hi + 1) * QC],
                                                           func=AF.Abs),
                        reads=[f"g2w{G}", "stat2"], writes=["stat2"])
                    if hi == 0:
                        T.v(lambda e: e.tensor_copy(out=stat_t[:], in_=stat2[:]),
                            reads=["stat2", "stat_t"], writes=["stat_t"])
                    else:
                        T.v(lambda e: e.tensor_max(stat_t[:], stat_t[:], stat2[:]),
                            reads=["stat2", "stat_t"], writes=["stat_t"])
                if hg == 0:
                    T.v(lambda e, qc=qc: e.tensor_copy(out=maxacc[:, qc * QC:(qc + 1) * QC],
                                                       in_=stat_t[:]),
                        reads=["stat_t", "maxacc"], writes=["maxacc"])
                else:
                    T.v(lambda e, qc=qc: e.tensor_max(maxacc[:, qc * QC:(qc + 1) * QC],
                                                      maxacc[:, qc * QC:(qc + 1) * QC], stat_t[:]),
                        reads=["stat_t", "maxacc"], writes=["maxacc"])
        alpha_finalize(1)
        for qc in range(NQC):
            H = qc % 2
            for hg in range(8):
                g = hg % 2
                T.dma("sync", f"g2i{g}", lambda e, hg=hg, qc=qc, base=base, g=g: e.dma_start(
                    out=g2w[g][:].rearrange("p (h n) -> p h n", h=4),
                    in_=g2_d.rearrange("(h p) n -> p h n", h=32)[:, hg * 4:(hg + 1) * 4,
                                                                base + qc * QC: base + (qc + 1) * QC]),
                    reads=[f"g2w{g}", "g2_d+"], writes=[f"g2w{g}"])
                for hi in range(4):
                    T.v(lambda e, g=g, qc=qc, hi=hi: e.tensor_mul(
                        g2w[g][:, hi * QC:(hi + 1) * QC], g2w[g][:, hi * QC:(hi + 1) * QC],
                        alpha_b[:, qc * QC:(qc + 1) * QC]),
                        reads=[f"g2w{g}", "alpha_b"], writes=[f"g2w{g}"])
                T.v(lambda e, g=g, hg=hg, H=H: e.tensor_scalar(
                    out=hqcp[H][:, hg * 4 * QC:(hg + 1) * 4 * QC], in0=g2w[g][:],
                    scalar1=MAGIC, scalar2=MAGIC, op0=ALU.add, op1=ALU.subtract),
                    reads=[f"g2w{g}", f"hqc{H}"], writes=[f"hqc{H}"])
            for j in range(8):
                r = j % 2
                T.dma("sync", f"w2l{r}", lambda e, j=j, r=r: e.dma_start(
                    out=w2s[r][:].rearrange("p (k n) -> p k n", k=32),
                    in_=w2q_d.ap().rearrange("(k p) n -> p k n", k=32)[:, :, j * 128:(j + 1) * 128]),
                    reads=[f"w2s{r}"], writes=[f"w2s{r}"])
                bank = 2 + j % 2
                for k in range(32):
                    T.t(lambda e, r=r, k=k, bank=bank, H=H: e.matmul(
                        out=ps[bank][:], lhsT=w2s[r][:, k * 128:(k + 1) * 128],
                        rhs=hqcp[H][:, k * QC:(k + 1) * QC],
                        start=(k == 0), stop=(k == 31)),
                        reads=[f"w2s{r}", f"hqc{H}"], writes=[f"ps{bank}"])
                rx = j % 2
                T.dma("sync", f"prx{rx}", lambda e, j=j, rx=rx, base=base, qc=qc: e.dma_start(
                    out=xrot[rx][:, 0:QC], in_=x1T[j * 128:(j + 1) * 128, base + qc * QC: base + (qc + 1) * QC]),
                    reads=[f"xrot{rx}"], writes=[f"xrot{rx}"])
                T.v(lambda e, bank=bank, qc=qc: e.tensor_mul(stat_t[:], ps[bank][:],
                                                             gsc_b[:, qc * QC:(qc + 1) * QC]),
                    reads=[f"ps{bank}", "gsc_b", "stat_t"], writes=["stat_t"])
                T.v(lambda e, j=j, rx=rx: e.scalar_tensor_tensor(
                    out=xrot[rx][:, 0:QC], in0=stat_t[:], scalar=b2s[:, j:j + 1],
                    in1=xrot[rx][:, 0:QC], op0=ALU.add, op1=ALU.add),
                    reads=["stat_t", "b2s", f"xrot{rx}"], writes=[f"xrot{rx}"])
                T.dma("sync", f"out{rx}", lambda e, j=j, rx=rx, base=base, qc=qc: e.dma_start(
                    out=x2T[j * 128:(j + 1) * 128, base + qc * QC: base + (qc + 1) * QC],
                    in_=xrot[rx][:, 0:QC]),
                    reads=[f"xrot{rx}"], writes=[f"x2o{j}_{qc}"])
        T.barrier()
        mmstack.close()

    bstack.close()
    return


def prep_l0f(x1T_core, ins):
    d = {}
    d["x1T"] = x1T_core
    d["W1T"] = np.ascontiguousarray(ins["L0_ffn_W1"].T)
    d["W2T"] = np.ascontiguousarray(ins["L0_ffn_W2"].T)
    d["b1_cols"] = np.ascontiguousarray(ins["L0_ffn_b1"].reshape(32, 128).T)
    d["b2_cols"] = np.ascontiguousarray(ins["L0_ffn_b2"].reshape(8, 128).T)
    d["ident_f"] = np.eye(128, dtype=np.float32)
    return d


F32 = mybir.dt.float32
BF16 = mybir.dt.bfloat16
AF = mybir.ActivationFunctionType
ALU = mybir.AluOpType

D = 1024
DI = 2048
NS = 16
TB = 2048
NB = 2
QC = 512
NQC = TB // QC
TC = 512
NTC = TB // TC


def emit_l1attn(nc, T, ps, ins, x_in, x_out, n_cores=8):
    import contextlib as _ctx
    bstack = _ctx.ExitStack()
    ent = bstack.enter_context
    x2T = x_in
    x3T = x_out
    xpT = ins['xpT']
    dtT = ins['dtT']
    bcT = ins['bcT']
    dtb_cols = ins['dtb_cols']
    alog_cols = ins['alog_cols']
    d_cols = ins['d_cols']
    cw_cols = ins['cw_cols']
    cb_cols = ins['cb_cols']
    outT = ins['outT']
    ident_in = ins['ident_f']
    ddiag_in = ins['ddiag']

    xi_d = nc.dram_tensor("l1a_xi_d", [DI, NB * TB], F32, kind="Internal").ap()
    xc_d = nc.dram_tensor("l1a_xc_d", [DI, NB * TB], F32, kind="Internal").ap()
    dl_d = nc.dram_tensor("l1a_dl_d", [DI, NB * TB], F32, kind="Internal").ap()
    y_d = nc.dram_tensor("l1a_y_d", [DI, NB * TB], BF16, kind="Internal").ap()
    bc_d = nc.dram_tensor("l1a_bc_d", [NB, TB * 32], F32, kind="Internal")

    sb = lambda name, shape, dt: ent(nc.sbuf_tensor('l1a_' + name, shape, dt))

    xrot = [sb(f"xrot{i}", [128, TB], F32) for i in range(2)]
    xbrot = [sb(f"xbrot{i}", [128, QC], BF16) for i in range(2)]
    xqrot = [sb(f"xqrot{i}", [128, QC], BF16) for i in range(2)]
    stat_t = sb("stat_t", [128, QC], F32)
    wrot = [sb(f"wrot{i}", [128, 8 * 128], BF16) for i in range(2)]
    bcw = sb("bcw", [128, 256], BF16)
    ev = [sb(f"ev{i}", [128, QC], F32) for i in range(2)]
    sp1 = sb("sp1", [128, QC], F32)
    bcc = sb("bcc", [32, QC], F32)
    hstate = sb("hstate", [128, 16 * NS], F32)
    cva = sb("cva", [128, TB], F32)
    acol = sb("acol", [128, 256], F32)
    dtbs = sb("dtbs", [128, 16], F32)
    dcs = sb("dcs", [128, 16], F32)
    cws = sb("cws", [128, 64], F32)
    cbs = sb("cbs", [128, 16], F32)
    ident_f = sb("identf", [128, 128], F32)
    ident_b = sb("identb", [128, 128], BF16)
    ddg = sb("ddg", [128, 16 * 128], BF16)
    ddgf = sb("ddgf", [128, 16 * 128], F32)
    ones_bf = sb("ones_bf", [128, 128], BF16)
    konst = sb("konst", [128, 2], F32)
    ps = [ent(nc.psum_tensor(f"l1a_ps{i}", [128, 512], F32)) for i in range(8)]


    for ap_in, buf, nm in ((dtb_cols, dtbs, "dtbs"), (alog_cols, acol, "acol"),
                           (d_cols, dcs, "dcs"), (cw_cols, cws, "cws"),
                           (cb_cols, cbs, "cbs"), (ident_in, ident_f, "identf")):
        T.dma("sync", "const", lambda e, ap_in=ap_in, buf=buf: e.dma_start(out=buf[:], in_=ap_in),
              writes=[nm])
    T.dma("sync", "const", lambda e: e.dma_start(out=ddgf[:], in_=ddiag_in), writes=["ddgf"])
    T.v(lambda e: e.tensor_copy(out=ddg[:], in_=ddgf[:]), reads=["ddgf", "ddg"], writes=["ddg"])
    T.v(lambda e: e.tensor_copy(out=ident_b[:], in_=ident_f[:]), reads=["identf", "identb"], writes=["identb"])
    T.g(lambda e: e.memset(ones_bf[:], 1.0), writes=["ones_bf"])
    T.g(lambda e: e.memset(konst[:, 0:1], 1e-5), writes=["konst"])
    T.g(lambda e: e.memset(konst[:, 1:2], 1.0), reads=["konst"], writes=["konst"])
    T.s(lambda e: e.activation(out=acol[:], in_=acol[:], func=AF.Exp), reads=["acol"], writes=["acol"])
    T.v(lambda e: e.tensor_scalar_mul(out=acol[:], in0=acol[:], scalar1=-1.0),
        reads=["acol"], writes=["acol"])
    T.dma("gpsimd", "bcwl", lambda e: e.dma_start(
        out=bcw[:].rearrange("p (j n) -> p j n", j=8),
        in_=bcT.rearrange("(j p) n -> p j n", j=8)),
        writes=["bcw"])

    import contextlib as _ctx

    for b in range(NB):
        base = b * TB
        pstack = _ctx.ExitStack()
        psb_ = pstack.enter_context
        xn_b = psb_(nc.sbuf_tensor(f"l1a_xn_b{b}", [128, 8 * TB], BF16))
        mean_s = psb_(nc.sbuf_tensor(f"l1a_mean{b}", [128, TB], F32))
        rstd_s = psb_(nc.sbuf_tensor(f"l1a_rstd{b}", [128, TB], F32))
        for j in range(8):
            r = j % 2
            T.dma("sync", f"lnx{r}", lambda e, j=j, r=r, base=base: e.dma_start(
                out=xrot[r][:], in_=x2T[j * 128:(j + 1) * 128, base:base + TB]),
                reads=[f"xrot{r}"], writes=[f"xrot{r}"])
            for ncc in range(4):
                rr = (j * 4 + ncc) % 2
                T.v(lambda e, r=r, rr=rr, ncc=ncc: e.tensor_copy(
                    out=xbrot[rr][:], in_=xrot[r][:, ncc * QC:(ncc + 1) * QC]),
                    reads=[f"xrot{r}", f"xb{rr}"], writes=[f"xb{rr}"])
                T.s(lambda e, r=r, rr=rr, ncc=ncc: e.activation(
                    out=xqrot[rr][:], in_=xrot[r][:, ncc * QC:(ncc + 1) * QC], func=AF.Square),
                    reads=[f"xrot{r}", f"xq{rr}"], writes=[f"xq{rr}"])
                T.t(lambda e, j=j, rr=rr, ncc=ncc: e.matmul(
                    out=ps[ncc][:], lhsT=ones_bf[:], rhs=xbrot[rr][:],
                    start=(j == 0), stop=(j == 7)),
                    reads=["ones_bf", f"xb{rr}"], writes=[f"ps{ncc}"])
                T.t(lambda e, j=j, rr=rr, ncc=ncc: e.matmul(
                    out=ps[4 + ncc][:], lhsT=ones_bf[:], rhs=xqrot[rr][:],
                    start=(j == 0), stop=(j == 7)),
                    reads=["ones_bf", f"xq{rr}"], writes=[f"ps{4 + ncc}"])
        for ncc in range(4):
            sl = slice(ncc * QC, (ncc + 1) * QC)
            T.s(lambda e, sl=sl, ncc=ncc: e.activation(
                out=mean_s[:, sl], in_=ps[ncc][:], func=AF.Identity, scale=1.0 / D),
                reads=[f"ps{ncc}", "mean_s"], writes=["mean_s"])
            T.s(lambda e, sl=sl, ncc=ncc: e.activation(
                out=rstd_s[:, sl], in_=ps[4 + ncc][:], func=AF.Identity, scale=1.0 / D),
                reads=[f"ps{4 + ncc}", "rstd_s"], writes=["rstd_s"])
            T.v(lambda e, sl=sl: e.tensor_mul(stat_t[:], mean_s[:, sl], mean_s[:, sl]),
                reads=["mean_s", "stat_t"], writes=["stat_t"])
            T.v(lambda e, sl=sl: e.tensor_sub(rstd_s[:, sl], rstd_s[:, sl], stat_t[:]),
                reads=["stat_t", "rstd_s"], writes=["rstd_s"])
            T.s(lambda e, sl=sl: e.activation(out=rstd_s[:, sl], in_=rstd_s[:, sl], func=AF.Ln,
                                              bias=konst[:, 0:1]),
                reads=["rstd_s", "konst"], writes=["rstd_s"])
            T.s(lambda e, sl=sl: e.activation(out=rstd_s[:, sl], in_=rstd_s[:, sl], func=AF.Exp,
                                              scale=-0.5),
                reads=["rstd_s"], writes=["rstd_s"])
        for j in range(8):
            r = j % 2
            T.dma("sync", f"lnx{r}", lambda e, j=j, r=r, base=base: e.dma_start(
                out=xrot[r][:], in_=x2T[j * 128:(j + 1) * 128, base:base + TB]),
                reads=[f"xrot{r}"], writes=[f"xrot{r}"])
            T.v(lambda e, r=r: e.tensor_sub(cva[:], xrot[r][:], mean_s[:]),
                reads=[f"xrot{r}", "mean_s", "cva"], writes=["cva"])
            T.v(lambda e, j=j: e.tensor_mul(xn_b[:, j * TB:(j + 1) * TB], cva[:], rstd_s[:]),
                reads=["cva", "rstd_s"], writes=[f"xn{j}"])

        for which, wsrc, tag in ((0, xpT, "xi"), (1, dtT, "dl")):
            for ct in range(16):
                r = ct % 2
                T.dma("gpsimd", f"pw{r}", lambda e, wsrc=wsrc, ct=ct, r=r: e.dma_start(
                    out=wrot[r][:].rearrange("p (j n) -> p j n", j=8),
                    in_=wsrc.rearrange("(j p) n -> p j n", j=8)[:, :, ct * 128:(ct + 1) * 128]),
                    reads=[f"wrot{r}"], writes=[f"wrot{r}"])
                for qc in range(NQC):
                    bank = qc % 2
                    for j in range(8):
                        T.t(lambda e, r=r, j=j, qc=qc, bank=bank: e.matmul(
                            out=ps[bank][:], lhsT=wrot[r][:, j * 128:(j + 1) * 128],
                            rhs=xn_b[:, j * TB + qc * QC: j * TB + (qc + 1) * QC],
                            start=(j == 0), stop=(j == 7)),
                            reads=[f"wrot{r}", f"xn{j}"], writes=[f"ps{bank}"])
                    g = qc % 2
                    if which == 0:
                        T.s(lambda e, bank=bank, g=g: e.copy(out=ev[g][:], in_=ps[bank][:]),
                            reads=[f"ps{bank}", f"ev{g}"], writes=[f"ev{g}"])
                        T.dma("sync", f"xio{g}", lambda e, ct=ct, qc=qc, base=base, g=g: e.dma_start(
                            out=xi_d[ct * 128:(ct + 1) * 128, base + qc * QC: base + (qc + 1) * QC],
                            in_=ev[g][:]),
                            reads=[f"ev{g}"], writes=["xi_d+"])
                    else:
                        T.s(lambda e, bank=bank, g=g, ct=ct: e.activation(
                            out=ev[g][:], in_=ps[bank][:], func=AF.Identity, bias=dtbs[:, ct:ct + 1]),
                            reads=[f"ps{bank}", "dtbs", f"ev{g}"], writes=[f"ev{g}"])
                        T.s(lambda e, g=g: e.activation(out=sp1[:], in_=ev[g][:], func=AF.Abs),
                            reads=[f"ev{g}", "sp1"], writes=["sp1"])
                        T.s(lambda e: e.activation(out=sp1[:], in_=sp1[:], func=AF.Exp, scale=-1.0),
                            reads=["sp1"], writes=["sp1"])
                        T.s(lambda e: e.activation(out=sp1[:], in_=sp1[:], func=AF.Ln,
                                                   bias=konst[:, 1:2]),
                            reads=["sp1", "konst"], writes=["sp1"])
                        T.v(lambda e, g=g: e.tensor_scalar_max(out=ev[g][:], in0=ev[g][:], scalar1=0.0),
                            reads=[f"ev{g}"], writes=[f"ev{g}"])
                        T.v(lambda e, g=g: e.tensor_add(ev[g][:], ev[g][:], sp1[:]),
                            reads=[f"ev{g}", "sp1"], writes=[f"ev{g}"])
                        T.dma("sync", f"dlo{g}", lambda e, ct=ct, qc=qc, base=base, g=g: e.dma_start(
                            out=dl_d[ct * 128:(ct + 1) * 128, base + qc * QC: base + (qc + 1) * QC],
                            in_=ev[g][:]),
                            reads=[f"ev{g}"], writes=["dl_d+"])

        for qc in range(NQC):
            bank = 4 + qc % 2
            for j in range(8):
                T.t(lambda e, j=j, qc=qc, bank=bank: e.matmul(
                    out=ps[bank][0:32, :], lhsT=bcw[:, j * 32:(j + 1) * 32],
                    rhs=xn_b[:, j * TB + qc * QC: j * TB + (qc + 1) * QC],
                    start=(j == 0), stop=(j == 7)),
                    reads=["bcw", f"xn{j}"], writes=[f"ps{bank}"])
            T.v(lambda e, bank=bank: e.tensor_copy(out=bcc[:], in_=ps[bank][0:32, :]),
                reads=[f"ps{bank}", "bcc"], writes=["bcc"])
            T.dma("sync", "bco", lambda e, b=b, qc=qc: e.dma_start(
                out=AP(bc_d, b * 32 * TB + qc * QC, [[TB, 32], [1, QC]]),
                in_=bcc[:]),
                reads=["bcc"], writes=["bc_d+"])

        T.barrier()
        pstack.close()
        pstack = _ctx.ExitStack()
        psb_ = pstack.enter_context
        cvt = psb_(nc.sbuf_tensor(f"l1a_cvt{b}", [128, TB], F32))
        hb = psb_(nc.sbuf_tensor(f"l1a_hb{b}", [128, TC * NS], F32))
        wbf = psb_(nc.sbuf_tensor(f"l1a_wbf{b}", [128, TC * NS], BF16))
        xcbf = psb_(nc.sbuf_tensor(f"l1a_xcbf{b}", [128, TC], BF16))
        Bb = psb_(nc.sbuf_tensor(f"l1a_Bb{b}", [128, TC * NS], F32))
        Cb = psb_(nc.sbuf_tensor(f"l1a_Cb{b}", [128, TC * NS], F32))
        dlt = [psb_(nc.sbuf_tensor(f"l1a_dlt{b}{i}", [128, TC], F32)) for i in range(2)]
        xct = [psb_(nc.sbuf_tensor(f"l1a_xct{b}{i}", [128, TC], F32)) for i in range(2)]
        du = psb_(nc.sbuf_tensor(f"l1a_du{b}", [128, TC], F32))
        asb = [psb_(nc.sbuf_tensor(f"l1a_asb{b}{i}", [128, TC], F32)) for i in range(2)]
        usb = [psb_(nc.sbuf_tensor(f"l1a_usb{b}{i}", [128, TC], F32)) for i in range(2)]
        yaccp = [psb_(nc.sbuf_tensor(f"l1a_yacc{b}{i}", [128, TC], F32)) for i in range(2)]
        for ct in range(16):
            T.dma("sync", "cvi", lambda e, ct=ct, base=base: e.dma_start(
                out=cvt[:], in_=xi_d[ct * 128:(ct + 1) * 128, base:base + TB]),
                reads=["cvt", "xi_d+"], writes=["cvt"])
            T.v(lambda e, ct=ct: e.tensor_scalar_mul(
                out=cva[:], in0=cvt[:], scalar1=cws[:, ct * 4 + 3: ct * 4 + 4]),
                reads=["cvt", "cws", "cva"], writes=["cva"])
            for k in range(3):
                off = 3 - k
                T.v(lambda e, ct=ct, k=k, off=off: e.scalar_tensor_tensor(
                    out=cva[:, off:TB], in0=cvt[:, 0:TB - off],
                    scalar=cws[:, ct * 4 + k: ct * 4 + k + 1],
                    in1=cva[:, off:TB], op0=ALU.mult, op1=ALU.add),
                    reads=["cvt", "cws", "cva"], writes=["cva"])
            T.s(lambda e, ct=ct: e.activation(out=cva[:], in_=cva[:], func=AF.Silu,
                                              bias=cbs[:, ct:ct + 1]),
                reads=["cva", "cbs"], writes=["cva"])
            T.dma("sync", "cvo", lambda e, ct=ct, base=base: e.dma_start(
                out=xc_d[ct * 128:(ct + 1) * 128, base:base + TB], in_=cva[:]),
                reads=["cva"], writes=["xc_d+"])

        for tc in range(NTC):
            t0 = base + tc * TC
            T.dma("sync", "bbc", lambda e, b=b, tc=tc: e.dma_start(
                out=Bb[:], in_=AP(bc_d, b * 32 * TB + tc * TC, [[0, 128], [TB, NS], [1, TC]])),
                reads=["bc_d+", "Bb"], writes=["Bb"])
            T.dma("sync", "cbc", lambda e, b=b, tc=tc: e.dma_start(
                out=Cb[:], in_=AP(bc_d, b * 32 * TB + 16 * TB + tc * TC, [[0, 128], [TB, NS], [1, TC]])),
                reads=["bc_d+", "Cb"], writes=["Cb"])
            for ct in range(16):
                r = ct % 2
                P = ct % 2
                ybank = 2 + P
                T.dma("sync", f"sdl{r}", lambda e, ct=ct, t0=t0, r=r: e.dma_start(
                    out=dlt[r][:], in_=dl_d[ct * 128:(ct + 1) * 128, t0:t0 + TC]),
                    reads=[f"dlt{r}", "dl_d+"], writes=[f"dlt{r}"])
                T.dma("sync", f"sxc{r}", lambda e, ct=ct, t0=t0, r=r: e.dma_start(
                    out=xct[r][:], in_=xc_d[ct * 128:(ct + 1) * 128, t0:t0 + TC]),
                    reads=[f"xct{r}", "xc_d+"], writes=[f"xct{r}"])
                T.v(lambda e, r=r: e.tensor_mul(du[:], dlt[r][:], xct[r][:]),
                    reads=[f"dlt{r}", f"xct{r}", "du"], writes=["du"])
                T.s(lambda e, r=r: e.copy(out=xcbf[:], in_=xct[r][:]),
                    reads=[f"xct{r}", "xcbf"], writes=["xcbf"])
                for s in range(NS):
                    w = s % 2
                    T.s(lambda e, r=r, ct=ct, s=s, w=w: e.activation(
                        out=asb[w][:], in_=dlt[r][:], func=AF.Exp,
                        scale=acol[:, ct * NS + s: ct * NS + s + 1]),
                        reads=[f"dlt{r}", "acol", f"asb{w}"], writes=[f"asb{w}"])
                    T.v(lambda e, s=s, w=w: e.tensor_mul(
                        usb[w][:], du[:], Bb[:, s * TC:(s + 1) * TC]),
                        reads=["du", "Bb", f"usb{w}"], writes=[f"usb{w}"])
                    if tc == 0:
                        T.v(lambda e, s=s, w=w: e.tensor_tensor_scan(
                            out=hb[:, s * TC:(s + 1) * TC], data0=asb[w][:], data1=usb[w][:],
                            initial=0.0, op0=ALU.mult, op1=ALU.add),
                            reads=[f"asb{w}", f"usb{w}", f"hb{s}"], writes=[f"hb{s}"])
                    else:
                        T.v(lambda e, ct=ct, s=s, w=w: e.tensor_tensor_scan(
                            out=hb[:, s * TC:(s + 1) * TC], data0=asb[w][:], data1=usb[w][:],
                            initial=hstate[:, ct * NS + s: ct * NS + s + 1],
                            op0=ALU.mult, op1=ALU.add),
                            reads=[f"asb{w}", f"usb{w}", f"hb{s}", "hst"], writes=[f"hb{s}"])
                    T.v(lambda e, s=s: e.tensor_mul(
                        wbf[:, s * TC:(s + 1) * TC], hb[:, s * TC:(s + 1) * TC],
                        Cb[:, s * TC:(s + 1) * TC]),
                        reads=[f"hb{s}", "Cb", f"wb{s}"], writes=[f"wb{s}"])
                if tc < NTC - 1:
                    T.s(lambda e, ct=ct: e.copy(
                        out=hstate[:, ct * NS:(ct + 1) * NS],
                        in_=hb[:].rearrange("p (s t) -> p s t", s=NS)[:, :, TC - 1]),
                        reads=["hst"] + [f"hb{s}" for s in range(NS)], writes=["hst"])
                for s in range(NS):
                    T.t(lambda e, s=s, ybank=ybank: e.matmul(
                        out=ps[ybank][:], lhsT=ident_b[:], rhs=wbf[:, s * TC:(s + 1) * TC],
                        start=(s == 0), stop=False),
                        reads=[f"wb{s}", "identb"], writes=[f"ps{ybank}"])
                T.t(lambda e, ct=ct, ybank=ybank: e.matmul(
                    out=ps[ybank][:], lhsT=ddg[:, ct * 128:(ct + 1) * 128], rhs=xcbf[:],
                    start=False, stop=True),
                    reads=["xcbf", "ddg", "identb"], writes=[f"ps{ybank}"])
                T.s(lambda e, P=P, ybank=ybank: e.copy(out=yaccp[P][:], in_=ps[ybank][:]),
                    reads=[f"ps{ybank}", f"yacc{P}"], writes=[f"yacc{P}"])
                T.dma("gpsimd", f"yo{r}", lambda e, ct=ct, t0=t0, r=r, P=P: e.dma_start(
                    out=y_d[ct * 128:(ct + 1) * 128, t0:t0 + TC], in_=yaccp[P][:]),
                    reads=[f"yacc{P}"], writes=["y_d+"])

        T.barrier()
        pstack.close()
        pstack = _ctx.ExitStack()
        psb_ = pstack.enter_context
        ybuf = psb_(nc.sbuf_tensor(f"l1a_ybuf{b}", [128, 16 * QC], BF16))
        worot = [psb_(nc.sbuf_tensor(f"l1a_worot{b}{i}", [128, 16 * 128], BF16)) for i in range(2)]
        for qc in range(NQC):
            T.dma("sync", "yi", lambda e, qc=qc, base=base: e.dma_start(
                out=ybuf[:].rearrange("p (k n) -> p k n", k=16),
                in_=y_d.rearrange("(k p) n -> p k n", k=16)[:, :, base + qc * QC: base + (qc + 1) * QC]),
                reads=["ybuf", "y_d+"], writes=["ybuf"])
            for e_t in range(8):
                r = e_t % 2
                T.dma("gpsimd", f"ow{r}", lambda e, e_t=e_t, r=r: e.dma_start(
                    out=worot[r][:].rearrange("p (k n) -> p k n", k=16),
                    in_=outT.rearrange("(k p) n -> p k n", k=16)[:, :, e_t * 128:(e_t + 1) * 128]),
                    reads=[f"worot{r}"], writes=[f"worot{r}"])
                T.dma("sync", f"prx{r}", lambda e, e_t=e_t, r=r, base=base, qc=qc: e.dma_start(
                    out=xrot[r][:, 0:QC],
                    in_=x2T[e_t * 128:(e_t + 1) * 128, base + qc * QC: base + (qc + 1) * QC]),
                    reads=[f"xrot{r}"], writes=[f"xrot{r}"])
                bank = e_t % 2
                for ct in range(16):
                    T.t(lambda e, r=r, ct=ct, bank=bank: e.matmul(
                        out=ps[bank][:], lhsT=worot[r][:, ct * 128:(ct + 1) * 128],
                        rhs=ybuf[:, ct * QC:(ct + 1) * QC], start=(ct == 0), stop=(ct == 15)),
                        reads=[f"worot{r}", "ybuf"], writes=[f"ps{bank}"])
                T.v(lambda e, bank=bank, r=r: e.tensor_add(
                    xrot[r][:, 0:QC], ps[bank][:], xrot[r][:, 0:QC]),
                    reads=[f"ps{bank}", f"xrot{r}"], writes=[f"xrot{r}"])
                T.dma("sync", f"out{r}", lambda e, e_t=e_t, r=r, base=base, qc=qc: e.dma_start(
                    out=x3T[e_t * 128:(e_t + 1) * 128, base + qc * QC: base + (qc + 1) * QC],
                    in_=xrot[r][:, 0:QC]),
                    reads=[f"xrot{r}"], writes=["x3+"])
        T.barrier()
        pstack.close()

    bstack.close()
    return


def prep_l1a(x2T_core, ins):
    d = {}
    d["x2T"] = x2T_core
    d["xpT"] = np.ascontiguousarray(ins["L1_attn_x_proj_w"].T)
    d["dtT"] = np.ascontiguousarray(ins["L1_attn_dt_proj_w"].T)
    d["bcT"] = np.ascontiguousarray(np.concatenate(
        [ins["L1_attn_B_proj_w"].T, ins["L1_attn_C_proj_w"].T], axis=1))
    d["dtb_cols"] = np.ascontiguousarray(ins["L1_attn_dt_proj_b"].reshape(16, 128).T)
    d["alog_cols"] = np.ascontiguousarray(
        ins["L1_attn_A_log"].reshape(16, 128, 16).transpose(1, 0, 2).reshape(128, 256))
    d["d_cols"] = np.ascontiguousarray(ins["L1_attn_D"].reshape(16, 128).T)
    d["cw_cols"] = np.ascontiguousarray(
        ins["L1_attn_conv_w"][:, 0, :].reshape(16, 128, 4).transpose(1, 0, 2).reshape(128, 64))
    d["cb_cols"] = np.ascontiguousarray(ins["L1_attn_conv_b"].reshape(16, 128).T)
    d["outT"] = np.ascontiguousarray(ins["L1_attn_out_proj_w"].T)
    d["ident_f"] = np.eye(128, dtype=np.float32)
    dd = np.zeros((16, 128, 128), np.float32)
    Dv = ins["L1_attn_D"].astype(np.float32).reshape(16, 128)
    idx = np.arange(128)
    for ct in range(16):
        dd[ct, idx, idx] = Dv[ct]
    d["ddiag"] = np.ascontiguousarray(dd.reshape(2048, 128).reshape(16, 128, 128).transpose(1, 0, 2).reshape(128, 16 * 128))
    return d


F32 = mybir.dt.float32
BF16 = mybir.dt.bfloat16
AF = mybir.ActivationFunctionType
ALU = mybir.AluOpType

D = 1024
I = 2816
NI = I // 128
TB = 2048
NB = 2
QC = 512
NQC = TB // QC
MAGIC = float(np.float32(1.5 * 2 ** 23))


def emit_l1ffn(nc, T, ps, ins, x_in, x_out, n_cores=8):
    import contextlib as _ctx
    bstack = _ctx.ExitStack()
    ent = bstack.enter_context
    x3T = x_in
    x4T = x_out
    w_gate = ins['w_gate']
    w_up = ins['w_up']
    w_down = ins['w_down']
    clips = ins['clips']
    ident_in = ins['ident_f']
    identb_in = ins['ident_bf']

    wgT_d = nc.dram_tensor("wgT_d", [D, I], BF16, kind="Internal").ap()
    wuT_d = nc.dram_tensor("wuT_d", [D, I], BF16, kind="Internal").ap()
    wdT_d = nc.dram_tensor("wdT_d", [I, D], BF16, kind="Internal").ap()
    xn_d = nc.dram_tensor("l1f_xn_d", [D, NB * TB], F32, kind="Internal").ap()
    h_d = nc.dram_tensor("l1f_h_d", [I, NB * TB], F32, kind="Internal").ap()
    cc_in = nc.dram_tensor("l1f_cc_in", [128, 16], F32, kind="Internal").ap()
    cc_out = nc.dram_tensor("l1f_cc_out", [128, 16], F32, kind="Internal").ap()

    sb = lambda name, shape, dt: ent(nc.sbuf_tensor('l1f_' + name, shape, dt))

    xqb = sb("xqb", [128, 8 * TB], BF16)
    hqc = sb("hqc", [128, NI * QC], BF16)
    ws = [sb(f"ws{i}", [128, 8 * 128], BF16) for i in range(2)]
    wu = [sb(f"wu{i}", [128, 8 * 128], BF16) for i in range(2)]
    wd = [sb(f"wd{i}", [128, NI * 128], BF16) for i in range(2)]
    wqtr = sb("wqtr", [128, NI * 128], BF16)
    wrow = [sb(f"wrow{i}", [128, I], F32) for i in range(2)]
    xrot = [sb(f"xrot{i}", [128, TB], F32) for i in range(2)]
    xbrot = [sb(f"xbrot{i}", [128, QC], BF16) for i in range(2)]
    xqrot = [sb(f"xqrot{i}", [128, QC], BF16) for i in range(2)]
    mean_s = sb("mean_s", [128, TB], F32)
    rstd_s = sb("rstd_s", [128, TB], F32)
    stat_t = sb("stat_t", [128, QC], F32)
    maxacc = sb("maxacc", [128, TB], F32)
    tq = [sb(f"tq{i}", [128, TB], F32) for i in range(2)]
    g2t = [sb(f"g2t{i}", [128, QC], F32) for i in range(2)]
    upt = sb("upt", [128, QC], F32)
    rsg = sb("rsg", [128, NI], F32)
    rsu = sb("rsu", [128, NI], F32)
    rsd = sb("rsd", [128, 8], F32)
    rsg_t = sb("rsg_t", [128, NI], F32)
    rsu_t = sb("rsu_t", [128, NI], F32)
    rsd_t = sb("rsd_t", [128, 8], F32)
    scl = sb("scl", [1, 8], F32)
    sccol = sb("sccol", [128, 2], F32)
    clp = sb("clp", [1, 2], F32)
    cct = sb("cct", [128, 16], F32)
    ident_f = sb("identf", [128, 128], F32)
    ident_b = sb("identb", [128, 128], BF16)
    ones_bf = sb("ones_bf", [128, 128], BF16)
    ones1_f = sb("ones1_f", [1, 128], F32)
    konst = sb("konst", [128, 1], F32)
    ps = [ent(nc.psum_tensor(f"l1f_ps{i}", [128, 512], F32)) for i in range(8)]


    T.dma("sync", "const", lambda e: e.dma_start(out=ident_f[:], in_=ident_in), writes=["identf"])
    T.dma("sync", "const", lambda e: e.dma_start(out=ident_b[:], in_=identb_in), writes=["identb"])
    T.dma("sync", "const", lambda e: e.dma_start(out=clp[:], in_=clips), writes=["clp"])
    T.g(lambda e: e.memset(ones_bf[:], 1.0), writes=["ones_bf"])
    T.g(lambda e: e.memset(ones1_f[:], 1.0), writes=["ones1_f"])
    T.g(lambda e: e.memset(konst[:], 1e-5), writes=["konst"])

    for wi, (wsrc, wdst, nrow_t, ncol_t, rsbuf, rsnm) in enumerate((
            (w_gate, wgT_d, NI, 8, rsg, "rsg"),
            (w_up, wuT_d, NI, 8, rsu, "rsu"),
            (w_down, wdT_d, 8, NI, rsd, "rsd"))):
        ncol = ncol_t * 128
        for it in range(nrow_t):
            rw = it % 2
            wr = wrow[rw]
            T.dma("sync", f"wql{rw}", lambda e, wsrc=wsrc, it=it, ncol=ncol, wr=wr: e.dma_start(
                out=wr[:, 0:ncol], in_=wsrc[it * 128:(it + 1) * 128, :]),
                reads=[f"wrow{rw}"], writes=[f"wrow{rw}"])
            T.v(lambda e, ncol=ncol, wr=wr: e.tensor_reduce(
                out=stat_t[:, 0:1], in_=wr[:, 0:ncol], axis=mybir.AxisListType.X,
                op=ALU.max, apply_absolute_value=True),
                reads=[f"wrow{rw}", "stat_t"], writes=["stat_t"])
            T.v(lambda e, it=it, rsbuf=rsbuf: e.tensor_scalar_mul(
                out=rsbuf[:, it:it + 1], in0=stat_t[:, 0:1], scalar1=1.0 / 127.0),
                reads=["stat_t", rsnm], writes=[rsnm])
            T.v(lambda e: e.reciprocal(out=stat_t[:, 1:2], in_=stat_t[:, 0:1]),
                reads=["stat_t"], writes=["stat_t"])
            T.v(lambda e, ncol=ncol, wr=wr: e.tensor_scalar(
                out=wr[:, 0:ncol], in0=wr[:, 0:ncol], scalar1=stat_t[:, 1:2], scalar2=127.0,
                op0=ALU.mult, op1=ALU.mult),
                reads=[f"wrow{rw}", "stat_t"], writes=[f"wrow{rw}"])
            T.v(lambda e, ncol=ncol, wr=wr: e.tensor_scalar(
                out=wr[:, 0:ncol], in0=wr[:, 0:ncol], scalar1=MAGIC, scalar2=MAGIC,
                op0=ALU.add, op1=ALU.subtract),
                reads=[f"wrow{rw}"], writes=[f"wrow{rw}"])
            T.v(lambda e, ncol=ncol, wr=wr: e.tensor_scalar(
                out=wr[:, 0:ncol], in0=wr[:, 0:ncol], scalar1=127.0, scalar2=-128.0,
                op0=ALU.min, op1=ALU.max),
                reads=[f"wrow{rw}"], writes=[f"wrow{rw}"])
            for j in range(ncol_t):
                pb = 5 + j % 2
                T.t(lambda e, j=j, wr=wr, pb=pb: e.transpose(out=ps[pb][:, 0:128],
                                               in_=wr[:, j * 128:(j + 1) * 128],
                                               identity=ident_f[:]),
                    reads=[f"wrow{rw}", "identf"], writes=[f"ps{pb}"])
                T.v(lambda e, j=j, pb=pb: e.tensor_copy(out=wqtr[:, j * 128:(j + 1) * 128],
                                                 in_=ps[pb][:, 0:128]),
                    reads=[f"ps{pb}", "wqtr"], writes=["wqtr"])
            T.dma("sync", "wqo", lambda e, wdst=wdst, it=it, ncol_t=ncol_t: e.dma_start(
                out=wdst.rearrange("(j p) n -> p j n", j=ncol_t)[:, :, it * 128:(it + 1) * 128],
                in_=wqtr[:, 0:ncol_t * 128].rearrange("p (j n) -> p j n", j=ncol_t)),
                reads=["wqtr"], writes=[f"wq{wi}+"])

    for b in range(NB):
        base = b * TB
        for j in range(8):
            r = j % 2
            T.dma("sync", f"lnx{r}", lambda e, j=j, r=r, base=base: e.dma_start(
                out=xrot[r][:], in_=x3T[j * 128:(j + 1) * 128, base:base + TB]),
                reads=[f"xrot{r}"], writes=[f"xrot{r}"])
            for ncc in range(4):
                rr = (j * 4 + ncc) % 2
                T.v(lambda e, r=r, rr=rr, ncc=ncc: e.tensor_copy(
                    out=xbrot[rr][:], in_=xrot[r][:, ncc * QC:(ncc + 1) * QC]),
                    reads=[f"xrot{r}", f"xb{rr}"], writes=[f"xb{rr}"])
                T.s(lambda e, r=r, rr=rr, ncc=ncc: e.activation(
                    out=xqrot[rr][:], in_=xrot[r][:, ncc * QC:(ncc + 1) * QC], func=AF.Square),
                    reads=[f"xrot{r}", f"xq{rr}"], writes=[f"xq{rr}"])
                T.t(lambda e, j=j, rr=rr, ncc=ncc: e.matmul(
                    out=ps[ncc][:], lhsT=ones_bf[:], rhs=xbrot[rr][:],
                    start=(j == 0), stop=(j == 7)),
                    reads=["ones_bf", f"xb{rr}"], writes=[f"ps{ncc}"])
                T.t(lambda e, j=j, rr=rr, ncc=ncc: e.matmul(
                    out=ps[4 + ncc][:], lhsT=ones_bf[:], rhs=xqrot[rr][:],
                    start=(j == 0), stop=(j == 7)),
                    reads=["ones_bf", f"xq{rr}"], writes=[f"ps{4 + ncc}"])
        for ncc in range(4):
            sl = slice(ncc * QC, (ncc + 1) * QC)
            T.s(lambda e, sl=sl, ncc=ncc: e.activation(
                out=mean_s[:, sl], in_=ps[ncc][:], func=AF.Identity, scale=1.0 / D),
                reads=[f"ps{ncc}", "mean_s"], writes=["mean_s"])
            T.s(lambda e, sl=sl, ncc=ncc: e.activation(
                out=rstd_s[:, sl], in_=ps[4 + ncc][:], func=AF.Identity, scale=1.0 / D),
                reads=[f"ps{4 + ncc}", "rstd_s"], writes=["rstd_s"])
            T.v(lambda e, sl=sl: e.tensor_mul(stat_t[:], mean_s[:, sl], mean_s[:, sl]),
                reads=["mean_s", "stat_t"], writes=["stat_t"])
            T.v(lambda e, sl=sl: e.tensor_sub(rstd_s[:, sl], rstd_s[:, sl], stat_t[:]),
                reads=["stat_t", "rstd_s"], writes=["rstd_s"])
            T.s(lambda e, sl=sl: e.activation(out=rstd_s[:, sl], in_=rstd_s[:, sl], func=AF.Ln,
                                              bias=konst[:, 0:1]),
                reads=["rstd_s", "konst"], writes=["rstd_s"])
            T.s(lambda e, sl=sl: e.activation(out=rstd_s[:, sl], in_=rstd_s[:, sl], func=AF.Exp,
                                              scale=-0.5),
                reads=["rstd_s"], writes=["rstd_s"])
        for j in range(8):
            r = j % 2
            T.dma("sync", f"lnx{r}", lambda e, j=j, r=r, base=base: e.dma_start(
                out=xrot[r][:], in_=x3T[j * 128:(j + 1) * 128, base:base + TB]),
                reads=[f"xrot{r}"], writes=[f"xrot{r}"])
            T.v(lambda e, r=r: e.tensor_sub(tq[0][:], xrot[r][:], mean_s[:]),
                reads=[f"xrot{r}", "mean_s", "tq0"], writes=["tq0"])
            T.v(lambda e: e.tensor_mul(tq[1][:], tq[0][:], rstd_s[:]),
                reads=["tq0", "rstd_s", "tq1"], writes=["tq1"])
            T.dma("sync", f"xno{r}", lambda e, j=j, base=base: e.dma_start(
                out=xn_d[j * 128:(j + 1) * 128, base:base + TB], in_=tq[1][:]),
                reads=["tq1"], writes=["xn_d+"])
            T.s(lambda e: e.activation(out=tq[0][:], in_=tq[1][:], func=AF.Abs),
                reads=["tq1", "tq0"], writes=["tq0"])
            if b == 0 and j == 0:
                T.v(lambda e: e.tensor_copy(out=maxacc[:], in_=tq[0][:]),
                    reads=["tq0", "maxacc"], writes=["maxacc"])
            else:
                T.v(lambda e: e.tensor_max(maxacc[:], maxacc[:], tq[0][:]),
                    reads=["tq0", "maxacc"], writes=["maxacc"])

    def global_scale(cid, clip_idx, out_col):
        T.v(lambda e: e.tensor_copy(out=tq[0][0:64, :], in_=maxacc[64:128, :]),
            reads=["maxacc", "tq0"], writes=["tq0"])
        T.v(lambda e: e.tensor_max(maxacc[0:64, :], maxacc[0:64, :], tq[0][0:64, :]),
            reads=["tq0", "maxacc"], writes=["maxacc"])
        T.v(lambda e: e.tensor_copy(out=tq[0][0:32, :], in_=maxacc[32:64, :]),
            reads=["maxacc", "tq0"], writes=["tq0"])
        T.v(lambda e: e.tensor_max(maxacc[0:32, :], maxacc[0:32, :], tq[0][0:32, :]),
            reads=["tq0", "maxacc"], writes=["maxacc"])
        T.v(lambda e: e.tensor_reduce(out=stat_t[0:32, 2:3], in_=maxacc[0:32, :],
                                      axis=mybir.AxisListType.X, op=ALU.max),
            reads=["maxacc", "stat_t"], writes=["stat_t"])
        T.t(lambda e: e.transpose(out=ps[6][0:1, 0:32], in_=stat_t[0:32, 2:3],
                                  identity=ident_f[0:32, 0:32]),
            reads=["stat_t", "identf"], writes=["ps6"])
        T.v(lambda e: e.tensor_reduce(out=scl[0:1, 0:1], in_=ps[6][0:1, 0:32],
                                      axis=mybir.AxisListType.X, op=ALU.max),
            reads=["ps6", "scl"], writes=["scl"])
        T.g(lambda e: e.memset(cct[:], 0.0), reads=["cct"], writes=["cct"])
        T.v(lambda e: e.tensor_copy(out=cct[0:1, 0:1], in_=scl[0:1, 0:1]),
            reads=["scl", "cct"], writes=["cct"])
        T.dma("sync", f"cci{cid}", lambda e: e.dma_start(out=cc_in, in_=cct[:]),
            reads=["cct"], writes=["ccin"])
        T.g(lambda e: e.collective_compute(
            "AllReduce", ALU.max, replica_groups=[list(range(n_cores))],
            ins=[cc_in.opt()], outs=[cc_out.opt()]),
            reads=["ccin"], writes=["ccout"], tag=f"cc{cid}")
        T.dma("sync", f"cco{cid}", lambda e: e.dma_start(out=cct[:], in_=cc_out),
            reads=["ccout", "cct"], writes=["cct"])
        T.s(lambda e, clip_idx=clip_idx: e.activation(out=scl[0:1, 1:2], in_=clp[0:1, clip_idx:clip_idx + 1],
                                                      func=AF.Sigmoid),
            reads=["clp", "scl"], writes=["scl"])
        T.v(lambda e: e.tensor_mul(scl[0:1, 2:3], scl[0:1, 1:2], cct[0:1, 0:1]),
            reads=["scl", "cct"], writes=["scl"])
        T.v(lambda e: e.tensor_scalar_mul(out=scl[0:1, 2:3], in0=scl[0:1, 2:3], scalar1=1.0 / 127.0),
            reads=["scl"], writes=["scl"])
        T.t(lambda e: e.matmul(out=ps[7][:, 0:1], lhsT=ones1_f[:], rhs=scl[0:1, 2:3],
                               start=True, stop=True),
            reads=["scl", "ones1_f"], writes=["ps7"])
        T.s(lambda e, out_col=out_col: e.copy(out=sccol[:, out_col + 0:out_col + 1] if False else stat_t[:, 3:4], in_=ps[7][:, 0:1]),
            reads=["ps7", "stat_t"], writes=["stat_t"])
        T.v(lambda e, out_col=out_col: e.reciprocal(out=sccol[:, out_col:out_col + 1], in_=ps[7][:, 0:1]),
            reads=["ps7", "sccol"], writes=["sccol"])
    global_scale(0, 0, 0)
    T.v(lambda e: e.tensor_scalar_mul(out=rsg_t[:], in0=rsg[:], scalar1=stat_t[:, 3:4]),
        reads=["rsg", "stat_t", "rsg_t"], writes=["rsg_t"])
    T.v(lambda e: e.tensor_scalar_mul(out=rsu_t[:], in0=rsu[:], scalar1=stat_t[:, 3:4]),
        reads=["rsu", "stat_t", "rsu_t"], writes=["rsu_t"])

    for b in range(NB):
        base = b * TB
        for j in range(8):
            r = j % 2
            T.dma("sync", f"xni{r}", lambda e, j=j, base=base, r=r: e.dma_start(
                out=tq[r][:], in_=xn_d[j * 128:(j + 1) * 128, base:base + TB]),
                reads=[f"tq{r}", "xn_d+"], writes=[f"tq{r}"])
            T.v(lambda e, r=r: e.tensor_scalar_mul(out=tq[r][:], in0=tq[r][:], scalar1=sccol[:, 0:1]),
                reads=[f"tq{r}", "sccol"], writes=[f"tq{r}"])
            T.v(lambda e, r=r: e.tensor_scalar(
                out=tq[r][:], in0=tq[r][:], scalar1=MAGIC, scalar2=MAGIC,
                op0=ALU.add, op1=ALU.subtract),
                reads=[f"tq{r}"], writes=[f"tq{r}"])
            T.v(lambda e, r=r, j=j: e.tensor_scalar(
                out=xqb[:, j * TB:(j + 1) * TB], in0=tq[r][:], scalar1=127.0, scalar2=-128.0,
                op0=ALU.min, op1=ALU.max),
                reads=[f"tq{r}", "xqb"], writes=["xqb"])
        for it in range(NI):
            r = it % 2
            T.dma("sync", f"wgl{r}", lambda e, it=it, r=r: e.dma_start(
                out=ws[r][:].rearrange("p (j n) -> p j n", j=8),
                in_=wgT_d.rearrange("(j p) n -> p j n", j=8)[:, :, it * 128:(it + 1) * 128]),
                reads=[f"ws{r}", "wq0+"], writes=[f"ws{r}"])
            T.dma("sync", f"wul{r}", lambda e, it=it, r=r: e.dma_start(
                out=wu[r][:].rearrange("p (j n) -> p j n", j=8),
                in_=wuT_d.rearrange("(j p) n -> p j n", j=8)[:, :, it * 128:(it + 1) * 128]),
                reads=[f"wu{r}", "wq1+"], writes=[f"wu{r}"])
            for qc in range(NQC):
                bank = qc % 2
                bank2 = 2 + qc % 2
                for j in range(8):
                    T.t(lambda e, r=r, j=j, qc=qc, bank=bank: e.matmul(
                        out=ps[bank][:], lhsT=ws[r][:, j * 128:(j + 1) * 128],
                        rhs=xqb[:, j * TB + qc * QC: j * TB + (qc + 1) * QC],
                        start=(j == 0), stop=(j == 7)),
                        reads=[f"ws{r}", "xqb"], writes=[f"ps{bank}"])
                for j in range(8):
                    T.t(lambda e, r=r, j=j, qc=qc, bank2=bank2: e.matmul(
                        out=ps[bank2][:], lhsT=wu[r][:, j * 128:(j + 1) * 128],
                        rhs=xqb[:, j * TB + qc * QC: j * TB + (qc + 1) * QC],
                        start=(j == 0), stop=(j == 7)),
                        reads=[f"wu{r}", "xqb"], writes=[f"ps{bank2}"])
                g = qc % 2
                T.s(lambda e, bank=bank, g=g, it=it: e.activation(
                    out=g2t[g][:], in_=ps[bank][:], func=AF.Silu, scale=rsg_t[:, it:it + 1]),
                    reads=[f"ps{bank}", "rsg_t", f"g2t{g}"], writes=[f"g2t{g}"])
                T.s(lambda e, bank2=bank2, it=it: e.activation(
                    out=upt[:], in_=ps[bank2][:], func=AF.Identity, scale=rsu_t[:, it:it + 1]),
                    reads=[f"ps{bank2}", "rsu_t", "upt"], writes=["upt"])
                T.v(lambda e, g=g: e.tensor_mul(g2t[g][:], g2t[g][:], upt[:]),
                    reads=[f"g2t{g}", "upt"], writes=[f"g2t{g}"])
                T.dma("sync", f"ho{g}", lambda e, it=it, qc=qc, base=base, g=g: e.dma_start(
                    out=h_d[it * 128:(it + 1) * 128, base + qc * QC: base + (qc + 1) * QC],
                    in_=g2t[g][:]),
                    reads=[f"g2t{g}"], writes=["h_d+"])
                T.s(lambda e, g=g: e.activation(out=stat_t[:], in_=g2t[g][:], func=AF.Abs),
                    reads=[f"g2t{g}", "stat_t"], writes=["stat_t"])
                if b == 0 and it == 0:
                    T.v(lambda e, qc=qc: e.tensor_copy(out=maxacc[:, qc * QC:(qc + 1) * QC],
                                                       in_=stat_t[:]),
                        reads=["stat_t", "maxacc"], writes=["maxacc"])
                else:
                    T.v(lambda e, qc=qc: e.tensor_max(maxacc[:, qc * QC:(qc + 1) * QC],
                                                      maxacc[:, qc * QC:(qc + 1) * QC], stat_t[:]),
                        reads=["stat_t", "maxacc"], writes=["maxacc"])

    global_scale(1, 1, 1)
    T.v(lambda e: e.tensor_scalar_mul(out=rsd_t[:], in0=rsd[:], scalar1=stat_t[:, 3:4]),
        reads=["rsd", "stat_t", "rsd_t"], writes=["rsd_t"])

    for b in range(NB):
        base = b * TB
        for qc in range(NQC):
            for it in range(NI):
                g = it % 2
                T.dma("sync", f"hi{g}", lambda e, it=it, qc=qc, base=base, g=g: e.dma_start(
                    out=g2t[g][:], in_=h_d[it * 128:(it + 1) * 128, base + qc * QC: base + (qc + 1) * QC]),
                    reads=[f"g2t{g}", "h_d+"], writes=[f"g2t{g}"])
                T.v(lambda e, g=g: e.tensor_scalar_mul(out=g2t[g][:], in0=g2t[g][:],
                                                       scalar1=sccol[:, 1:2]),
                    reads=[f"g2t{g}", "sccol"], writes=[f"g2t{g}"])
                T.v(lambda e, g=g: e.tensor_scalar(
                    out=g2t[g][:], in0=g2t[g][:], scalar1=MAGIC, scalar2=MAGIC,
                    op0=ALU.add, op1=ALU.subtract),
                    reads=[f"g2t{g}"], writes=[f"g2t{g}"])
                T.v(lambda e, g=g, it=it: e.tensor_scalar(
                    out=hqc[:, it * QC:(it + 1) * QC], in0=g2t[g][:], scalar1=127.0, scalar2=-128.0,
                    op0=ALU.min, op1=ALU.max),
                    reads=[f"g2t{g}", "hqc"], writes=["hqc"])
            for e_t in range(8):
                r = e_t % 2
                T.dma("sync", f"wdl{r}", lambda e, e_t=e_t, r=r: e.dma_start(
                    out=wd[r][:].rearrange("p (k n) -> p k n", k=NI),
                    in_=wdT_d.rearrange("(k p) n -> p k n", k=NI)[:, :, e_t * 128:(e_t + 1) * 128]),
                    reads=[f"wd{r}", "wq2+"], writes=[f"wd{r}"])
                T.dma("sync", f"prx{r}", lambda e, e_t=e_t, r=r, base=base, qc=qc: e.dma_start(
                    out=xrot[r][:, 0:QC],
                    in_=x3T[e_t * 128:(e_t + 1) * 128, base + qc * QC: base + (qc + 1) * QC]),
                    reads=[f"xrot{r}"], writes=[f"xrot{r}"])
                bank = 4 + e_t % 2
                for k in range(NI):
                    T.t(lambda e, r=r, k=k, bank=bank: e.matmul(
                        out=ps[bank][:], lhsT=wd[r][:, k * 128:(k + 1) * 128],
                        rhs=hqc[:, k * QC:(k + 1) * QC], start=(k == 0), stop=(k == NI - 1)),
                        reads=[f"wd{r}", "hqc"], writes=[f"ps{bank}"])
                T.s(lambda e, bank=bank, e_t=e_t: e.activation(
                    out=upt[:], in_=ps[bank][:], func=AF.Identity, scale=rsd_t[:, e_t:e_t + 1]),
                    reads=[f"ps{bank}", "rsd_t", "upt"], writes=["upt"])
                T.v(lambda e, r=r: e.tensor_add(xrot[r][:, 0:QC], upt[:], xrot[r][:, 0:QC]),
                    reads=["upt", f"xrot{r}"], writes=[f"xrot{r}"])
                T.dma("sync", f"out{r}", lambda e, e_t=e_t, r=r, base=base, qc=qc: e.dma_start(
                    out=x4T[e_t * 128:(e_t + 1) * 128, base + qc * QC: base + (qc + 1) * QC],
                    in_=xrot[r][:, 0:QC]),
                    reads=[f"xrot{r}"], writes=["x4+"])

    bstack.close()
    return


def prep_l1f(x3T_core, ins):
    d = {}
    d["x3T"] = x3T_core
    d["w_gate"] = np.ascontiguousarray(ins["L1_ffn_w_gate"])
    d["w_up"] = np.ascontiguousarray(ins["L1_ffn_w_up"])
    d["w_down"] = np.ascontiguousarray(ins["L1_ffn_w_down"])
    d["clips"] = np.ascontiguousarray(np.stack(
        [ins["L1_ffn_clip_x"][0], ins["L1_ffn_clip_down"][0]]).reshape(1, 2)).astype(np.float32)
    d["ident_f"] = np.eye(128, dtype=np.float32)
    d["ident_bf"] = np.eye(128, dtype=ml_dtypes.bfloat16)
    return d


F32 = mybir.dt.float32
BF16 = mybir.dt.bfloat16
D = 1024
TB = 2048
NB = 2


def build_fused(stack, n_cores=8):
    nc = bass.Bass("TRN2", target_bir_lowering=False)
    ent = stack.enter_context
    di = lambda name, shape, dt=F32: nc.dram_tensor(name, shape, dt, kind="ExternalInput").ap()

    xT = di("xT", [D, NB * TB])
    x4T = nc.dram_tensor("x4T", [D, NB * TB], F32, kind="ExternalOutput").ap()
    x1T = nc.dram_tensor("g_x1T", [D, NB * TB], F32, kind="Internal").ap()
    x2T = nc.dram_tensor("g_x2T", [D, NB * TB], F32, kind="Internal").ap()
    x3T = nc.dram_tensor("g_x3T", [D, NB * TB], F32, kind="Internal").ap()

    ins = {}
    for name, shape in (("w_qkv", [D, 3 * D]), ("bqk_cols", [128, 16]), ("bv_cols", [128, 8]),
                        ("bproj_row", [1, D]), ("w_proj", [D, D]), ("sinT", [64, 1369]),
                        ("cosT", [64, 1369]),
                        ("W1T", [D, 4096]), ("W2T", [4096, D]), ("b1_cols", [128, 32]),
                        ("b2_cols", [128, 8]), ("ident_f", [128, 128]),
                        ("xpT", [D, 2048]), ("dtT", [D, 2048]), ("bcT", [D, 32]),
                        ("dtb_cols", [128, 16]), ("alog_cols", [128, 256]), ("d_cols", [128, 16]),
                        ("cw_cols", [128, 64]), ("cb_cols", [128, 16]), ("outT", [2048, D]),
                        ("ddiag", [128, 16 * 128]),
                        ("w_gate", [2816, D]), ("w_up", [2816, D]), ("w_down", [D, 2816]),
                        ("clips", [1, 2])):
        ins[name] = di(name, shape)
    ins["ident_bf"] = di("ident_bf", [128, 128], BF16)

    T = Tape()
    emit_l0attn(nc, T, None, ins, xT, x1T, n_cores=n_cores)
    T.barrier()
    emit_l0ffn(nc, T, None, ins, x1T, x2T, n_cores=n_cores)
    T.barrier()
    emit_l1attn(nc, T, None, ins, x2T, x3T, n_cores=n_cores)
    T.barrier()
    emit_l1ffn(nc, T, None, ins, x3T, x4T, n_cores=n_cores)
    T.emit(nc, stack)
    return nc


def prep_fused(x_core, inputs):
    m = prep_l0a(x_core, inputs)
    m2 = prep_l0f(None, inputs); m2.pop("x1T", None)
    m3 = prep_l1a(None, inputs); m3.pop("x2T", None)
    m4 = prep_l1f(None, inputs); m4.pop("x3T", None)
    for d in (m2, m3, m4):
        m.update(d)
    return m



_CACHE = {}


def _get_fused():
    if "f" not in _CACHE:
        stack = contextlib.ExitStack()
        nc = build_fused(stack)
        _CACHE["f"] = (nc, stack)
    return _CACHE["f"][0]


def kernel(**inputs):
    n_cores = 8
    x = inputs["L0_attn_x"]
    nc = _get_fused()
    in_maps = [prep_fused(x[c * NB:(c + 1) * NB], inputs) for c in range(n_cores)]
    res = run_bass_kernel_spmd(nc, in_maps, core_ids=list(range(n_cores)))
    out = np.zeros((16, TB, D), np.float32)
    for c in range(n_cores):
        out[c * NB:(c + 1) * NB] = res.results[c]["x4T"].reshape(D, NB, TB).transpose(1, 2, 0)
    return out
